# version 01
# baseline (speedup 1.0000x reference)
import numpy as np
from contextlib import ExitStack

import concourse.bass as bass
import concourse.mybir as mybir
from concourse.bass_utils import run_bass_kernel_spmd

F32 = mybir.dt.float32
F32R = mybir.dt.float32r
BF16 = mybir.dt.bfloat16
AF = mybir.ActivationFunctionType
ALU = mybir.AluOpType
AX = mybir.AxisListType

D = 4096
NIC = 32
DFF = 11008
NFF = 86
SEQ = 4096
TOWN = 2048
HD = 128
NH = 32
NKV = 8
TT = 512
NTT = 4
RSQ = 0.08838834764831845
INV_D = 1.0 / D
EPS = 1e-5

ENGS = ("sync", "gpsimd", "vector", "scalar", "tensor")


class Sem:
    __slots__ = ("h", "n", "name")

    def __init__(self, h, name):
        self.h = h
        self.n = 0
        self.name = name


def build_program(upto=99, dump=()):
    nc = bass.Bass("TRN2", target_bir_lowering=False)

    def kindof(nm):
        return "ExternalOutput" if nm in dump else "Internal"

    x_own = nc.dram_tensor("x_own", [D, TOWN], F32, kind="ExternalInput").ap()
    x_oth = nc.dram_tensor("x_oth", [D, TOWN], F32, kind="ExternalInput").ap()
    wq = nc.dram_tensor("wq", [NH, 128, NIC, 128], F32, kind="ExternalInput").ap()
    wk = nc.dram_tensor("wk", [NKV, 128, NIC, 128], F32, kind="ExternalInput").ap()
    wv = nc.dram_tensor("wv", [NIC, 128, 1024], F32, kind="ExternalInput").ap()
    wo = nc.dram_tensor("wo", [NH, 128, NIC, 128], F32, kind="ExternalInput").ap()
    wg = nc.dram_tensor("wg", [NFF, 128, NIC, 128], F32, kind="ExternalInput").ap()
    wu = nc.dram_tensor("wu", [NFF, 128, NIC, 128], F32, kind="ExternalInput").ap()
    wd = nc.dram_tensor("wd", [NIC, 128, NFF, 128], F32, kind="ExternalInput").ap()
    qs = nc.dram_tensor("qs", [128, NH], F32, kind="ExternalInput").ap()
    ks = nc.dram_tensor("ks", [128, NKV], F32, kind="ExternalInput").ap()
    vs = nc.dram_tensor("vs", [1, 1024], F32, kind="ExternalInput").ap()
    os_ = nc.dram_tensor("os", [128, NH], F32, kind="ExternalInput").ap()
    out_t = nc.dram_tensor("out", [D, TOWN], F32, kind="ExternalOutput").ap()

    q_s = nc.dram_tensor("q_s", [NH, HD, TOWN], BF16, kind=kindof("q_s")).ap()
    k_s = nc.dram_tensor("k_s", [NKV, HD, SEQ], BF16, kind=kindof("k_s")).ap()
    v_s = nc.dram_tensor("v_s", [NKV, NIC, 128, HD], BF16, kind=kindof("v_s")).ap()
    o_s = nc.dram_tensor("o_s", [NTT, NIC, 128, TT], F32, kind=kindof("o_s")).ap()
    x2_s = nc.dram_tensor("x2_s", [NTT, NIC, 128, TT], F32, kind=kindof("x2_s")).ap()
    h_s = nc.dram_tensor("h_s", [NTT, 128, NFF, TT], BF16, kind=kindof("h_s")).ap()
    y_s = nc.dram_tensor("y_s", [NTT, NIC, 128, TT], F32, kind=kindof("y_s")).ap()
    xn_d = nc.dram_tensor("xn_d", [NIC, 128, TOWN], BF16,
                          kind="ExternalOutput").ap() if "xn_d" in dump else None
    rows_d = nc.dram_tensor("rows_d", [16, 1, 1536], F32,
                            kind="ExternalOutput").ap() if "rows_d" in dump else None
    e2_d = nc.dram_tensor("e2_d", [16, 1, 512], F32,
                          kind="ExternalOutput").ap() if "rows_d" in dump else None
    swv_d = nc.dram_tensor("swv_d", [4, 128, 16384], BF16,
                           kind="ExternalOutput").ap() if "swv_d" in dump else None
    xn2_d = nc.dram_tensor("xn2_d", [NIC, 128, TOWN], BF16,
                           kind="ExternalOutput").ap() if "xn2_d" in dump else None
    swv_i = [0]
    rows_i = [0]

    with ExitStack() as stk:
        e_ = stk.enter_context
        XN = e_(nc.sbuf_tensor("XN", [128, 65536], BF16))
        A2 = e_(nc.sbuf_tensor("A2", [128, 17408], F32))
        ROWS = e_(nc.sbuf_tensor("ROWS", [1, 2048], F32))
        ONESF = e_(nc.sbuf_tensor("ONESF", [128, 1], F32))
        ONESB = e_(nc.sbuf_tensor("ONESB", [128, 1], BF16))
        CONSTT = e_(nc.sbuf_tensor("CONSTT", [128, 32], F32))
        ONES1 = e_(nc.sbuf_tensor("ONES1", [1, 128], F32))
        EPS1 = e_(nc.sbuf_tensor("EPS1", [1, 1], F32))
        NEGC = e_(nc.sbuf_tensor("NEGC", [128, 1], F32))
        QS = e_(nc.sbuf_tensor("QS", [128, NH], F32))
        KS = e_(nc.sbuf_tensor("KS", [128, NKV], F32))
        OS = e_(nc.sbuf_tensor("OS", [128, NH], F32))
        P = e_(nc.psum_tensor("P", [128, 4096], F32))

        A2f = A2[:, :]
        A2r = A2.bitcast(F32R)
        A2b = A2.bitcast(BF16)
        XNa = XN[:, :]

        def xncol(ic, c0, n):
            return XNa[:, ic * TOWN + c0: ic * TOWN + c0 + n]

        sems = {}
        for name in ("sBar", "sPE", "sDVE", "sACT", "sLD0", "sLD1", "sGC0", "sGC1",
                     "sST", "sXT", "sQL", "sKV", "sLDA"):
            sems[name] = Sem(e_(nc.semaphore(name)), name)
        sBar, sPE, sDVE, sACT = (sems["sBar"], sems["sPE"], sems["sDVE"],
                                 sems["sACT"])
        sLD = [sems["sLD0"], sems["sLD1"]]
        sGC = [sems["sGC0"], sems["sGC1"]]
        sST, sXT, sQL, sKV, sLDA = (sems["sST"], sems["sXT"], sems["sQL"],
                                    sems["sKV"], sems["sLDA"])

        REC = {e: [] for e in ENGS}

        def W(eng, sem, n):
            if n > 0:
                REC[eng].append(lambda e, h=sem.h, v=n: e.wait_ge(h, v))

        def OP(eng, meth, inc=None, amt=1, **kw):
            if inc is not None:
                inc.n += amt

            def f(e, meth=meth, kw=kw, inc=inc, amt=amt):
                r = getattr(e, meth)(**kw)
                if inc is not None:
                    r.then_inc(inc.h, amt)
            REC[eng].append(f)
            return inc.n if inc is not None else 0

        def mm(out, lhsT, rhs, start, stop, inc=None, skip=False):
            return OP("tensor", "matmul", inc=inc, out=out, lhsT=lhsT, rhs=rhs,
                      start=start, stop=stop, skip_group_check=skip)

        def actfence():
            return OP("scalar", "activation", inc=sACT, out=CONSTT[0:1, 4:5],
                      in_=CONSTT[0:1, 0:1], func=AF.Copy)

        nbar = [0]

        def barrier():
            W("vector", sDVE, sDVE.n)
            W("scalar", sACT, sACT.n)
            W("tensor", sPE, sPE.n)
            W("gpsimd", sST, sST.n)
            for s in (sGC[0], sGC[1]):
                W("gpsimd", s, s.n)
            for s in (sLD[0], sLD[1], sXT, sKV, sQL, sLDA):
                W("sync", s, s.n)
            nbar[0] += 1
            target = 5 * nbar[0]
            sBar.n = target
            for eng in ENGS:
                REC[eng].append(lambda e, h=sBar.h: e.sem_inc(h, 1))
                REC[eng].append(lambda e, h=sBar.h, v=target: e.wait_ge(h, v))

        c0 = OP("vector", "memset", inc=sDVE, ap=CONSTT[:, :], constant=1.0)
        W("vector", sDVE, c0)
        OP("vector", "tensor_copy", inc=sDVE, out=ONESF[:, :], in_=CONSTT[:, 0:1])
        for _j in range(4):
            OP("vector", "tensor_copy", inc=sDVE, out=ONES1[0:1, _j * 32:(_j + 1) * 32],
               in_=CONSTT[0:1, :])
        OP("vector", "memset", inc=sDVE, ap=ONESB[:, :], constant=1.0)
        OP("vector", "memset", inc=sDVE, ap=EPS1[:, :], constant=EPS)
        OP("sync", "dma_start", inc=sLDA, amt=16, out=QS[:, :], in_=qs)
        OP("sync", "dma_start", inc=sLDA, amt=16, out=KS[:, :], in_=ks)
        OP("sync", "dma_start", inc=sLDA, amt=16, out=OS[:, :], in_=os_)
        consts_dve = sDVE.n
        consts_ld = sLDA.n
        for eng in ("tensor", "vector", "scalar"):
            W(eng, sDVE, consts_dve)
            W(eng, sLDA, consts_ld)
        barrier()

        def ln_tile(load_fn, xn_c0):
            XT = A2f[:, 0:16384]
            ldn = load_fn()
            sq = [A2f[:, 16384 + i * 512: 16384 + (i + 1) * 512] for i in range(2)]
            sq_dve = [0, 0]
            mm_pe = [0, 0]
            for fc in range(NIC):
                sl = fc % 2
                W("vector", sXT, ldn)
                if fc >= 2:
                    W("vector", sPE, mm_pe[sl])
                OP("vector", "tensor_tensor", inc=sDVE, out=sq[sl][:, :],
                   in0=XT[:, fc * 512:(fc + 1) * 512],
                   in1=XT[:, fc * 512:(fc + 1) * 512], op=ALU.mult)
                sq_dve[sl] = OP("vector", "tensor_copy", inc=sDVE,
                                out=CONSTT[0:1, 1:2], in_=CONSTT[0:1, 0:1])
                if fc == 0:
                    W("tensor", sXT, ldn)
                    W("tensor", sDVE, sq_dve[0])
                mm(P[0:1, 2048:2560], ONESF[:, :], XT[:, fc * 512:(fc + 1) * 512],
                   start=(fc == 0), stop=(fc == NIC - 1), skip=True)
                W("tensor", sDVE, sq_dve[sl])
                mm_pe[sl] = mm(P[0:1, 2560:3072], ONESF[:, :], sq[sl][:, :],
                               start=(fc == 0), stop=(fc == NIC - 1), inc=sPE,
                               skip=True)
            stats_done = sPE.n
            rm = ROWS[0:1, 0:512]
            rv = ROWS[0:1, 512:1024]
            rt = ROWS[0:1, 1024:1536]
            rvar = ROWS[0:1, 1536:2048]
            rln = ROWS[0:1, 512:1024]
            rstd = ROWS[0:1, 1024:1536]
            rt2 = ROWS[0:1, 1536:2048]
            W("vector", sPE, stats_done)
            c1 = OP("vector", "tensor_scalar", inc=sDVE, out=rm, in0=P[0:1, 2048:2560],
                    scalar1=INV_D, scalar2=None, op0=ALU.mult)
            c2 = OP("vector", "tensor_scalar", inc=sDVE, out=rv, in0=P[0:1, 2560:3072],
                    scalar1=INV_D, scalar2=None, op0=ALU.mult)
            W("vector", sDVE, c2)
            if e2_d is not None and rows_i[0] < 16:
                W("gpsimd", sDVE, c2)
                dmn = OP("gpsimd", "dma_start", inc=sST, amt=16,
                         out=e2_d[rows_i[0]], in_=ROWS[0:1, 512:1024])
                W("vector", sST, dmn)
            c3 = OP("vector", "tensor_tensor", inc=sDVE, out=rt, in0=rm, in1=rm,
                    op=ALU.mult)
            W("vector", sDVE, c3)
            c4 = OP("vector", "tensor_tensor", inc=sDVE, out=rvar, in0=rv, in1=rt,
                    op=ALU.subtract)
            W("scalar", sDVE, c4)
            aa = OP("scalar", "activation", inc=sACT, out=rln, in_=rvar, func=AF.Ln,
                    bias=EPS1[0:1, 0:1])
            W("scalar", sACT, aa)
            OP("scalar", "activation", inc=sACT, out=rstd, in_=rln, func=AF.Exp,
               scale=-0.5)
            a2 = actfence()
            W("vector", sACT, a2)
            c5 = OP("vector", "scalar_tensor_tensor", inc=sDVE, out=rt2, in0=rm,
                    scalar=-1.0, in1=rstd, op0=ALU.mult, op1=ALU.mult)
            if rows_d is not None and rows_i[0] < 16:
                W("gpsimd", sDVE, c5)
                OP("gpsimd", "dma_start", inc=sST, amt=16,
                   out=rows_d[rows_i[0]], in_=ROWS[0:1, 0:1536])
                rows_i[0] += 1
            W("tensor", sACT, a2)
            W("tensor", sDVE, c5)
            mm(P[:, 3072:3584], ONES1[:, :], rstd, start=True, stop=True)
            bc = mm(P[:, 3584:4096], ONES1[:, :], rt2, start=True, stop=True, inc=sPE)
            TMP = A2f[:, 16384:16896]
            last = 0
            for fc in range(NIC):
                W("vector", sPE, bc)
                if fc > 0:
                    W("vector", sDVE, last)
                t1 = OP("vector", "tensor_tensor", inc=sDVE, out=TMP,
                        in0=XT[:, fc * 512:(fc + 1) * 512], in1=P[:, 3072:3584],
                        op=ALU.mult)
                W("vector", sDVE, t1)
                last = OP("vector", "tensor_tensor", inc=sDVE,
                          out=xncol(fc, xn_c0, 512), in0=TMP, in1=P[:, 3584:4096],
                          op=ALU.add)
            return last

        def proj_fm(wt, noc, scl, store_fn):
            WA = [A2f[:, 0:4096], A2f[:, 4096:8192]]
            AB = A2f[:, 8192:9216]
            scol = A2f[:, 9216:9217]
            scol2 = A2f[:, 9217:9218]
            SG = [A2b[:, 20480:24576], A2b[:, 24576:28672]]
            ST = [A2b[:, 28672 + i * 512: 28672 + (i + 1) * 512] for i in range(2)]
            ld_n = [0, 0]
            sign_act = [0, 0]
            abs_dve = 0
            evac = {}
            store_n = {}
            proj_pe = {}
            for oc in range(noc):
                sl = oc % 2
                if oc >= 2:
                    W("sync", sACT, max(sign_act[sl], abs_dve))
                ld_n[sl] = OP("sync", "dma_start", inc=sLD[sl], amt=16,
                              out=WA[sl].rearrange("p (c k) -> p c k", c=NIC),
                              in_=wt[oc])
                W("scalar", sLD[sl], ld_n[sl])
                if oc >= 2:
                    W("scalar", sPE, proj_pe.get(oc - 2, 0))
                sign_act[sl] = OP("scalar", "activation", inc=sACT, out=SG[sl][:, :],
                                  in_=WA[sl][:, :], func=AF.Sign)
                qtr_pe = 0
                for qtr in range(4):
                    W("scalar", sPE, qtr_pe)
                    OP("scalar", "activation", inc=sACT, out=AB[:, :],
                       in_=WA[sl][:, qtr * 1024:(qtr + 1) * 1024],
                       func=AF.Abs)
                    abs_act = actfence()
                    W("tensor", sACT, abs_act)
                    for j in range(8):
                        ic = qtr * 8 + j
                        qtr_pe = mm(P[:, 2048:2049], AB[:, j * 128:(j + 1) * 128],
                                    ONESF[:, :], start=(ic == 0),
                                    stop=(ic == NIC - 1),
                                    inc=(sPE if j == 7 else None))
                abs_dve = abs_act
                sign_act[sl] = abs_act
                smm_done = sPE.n
                W("vector", sPE, smm_done)
                d1 = OP("vector", "tensor_scalar", inc=sDVE, out=scol,
                        in0=P[:, 2048:2049], scalar1=INV_D, scalar2=1e-8,
                        op0=ALU.mult, op1=ALU.max)
                W("vector", sDVE, d1)
                d2 = OP("vector", "tensor_tensor", inc=sDVE, out=scol2, in0=scol,
                        in1=scl[:, oc:oc + 1], op=ALU.mult)
                for tt in range(NTT):
                    bank = P[:, (tt % 2) * 512:(tt % 2) * 512 + 512]
                    W("tensor", sACT, sign_act[sl])
                    pk = (oc, tt - 2) if tt >= 2 else \
                        ((oc - 1, NTT - 2 + tt) if oc > 0 else None)
                    if pk is not None and pk in evac:
                        W("tensor", sDVE, evac[pk])
                    for ic in range(NIC):
                        proj_pe[oc] = mm(bank, SG[sl][:, ic * 128:(ic + 1) * 128],
                                         xncol(ic, tt * TT, TT), start=(ic == 0),
                                         stop=(ic == NIC - 1),
                                         inc=(sPE if ic == NIC - 1 else None))
                    W("vector", sPE, proj_pe[oc])
                    W("vector", sDVE, d2)
                    if pk is not None and pk in store_n:
                        W("vector", sST, store_n[pk])
                    evac[(oc, tt)] = OP("vector", "tensor_scalar", inc=sDVE,
                                        out=ST[tt % 2], in0=bank, scalar1=scol2,
                                        scalar2=None, op0=ALU.mult)
                    W("gpsimd", sDVE, evac[(oc, tt)])
                    store_n[(oc, tt)] = store_fn(oc, tt, ST[tt % 2])

        def stage_v(half):
            WV = [A2f[:, 8192 + i * 1024: 8192 + (i + 1) * 1024]
                  for i in range(4)]
            SGTMP = A2f[:, 14336:14848]
            ABv = A2f[:, 14848:15360]
            SWV = A2b[:, 0:16384]
            svb = A2f[:, 15360:15872]
            VSl = A2f[0:1, 16384:17408]
            STv = [A2b[:, 31744 + i * 512: 31744 + (i + 1) * 512] for i in range(2)]
            vsn = OP("sync", "dma_start", inc=sLDA, amt=16, out=VSl, in_=vs)
            for fb in range(2):
                ld_n = [0, 0, 0, 0]
                sg_act = 0
                ab_dve = 0
                smm = 0
                swv_free_pe = sPE.n
                for pf in range(3):
                    ld_n[pf] = OP("sync", "dma_start", inc=sLD[pf % 2], amt=16,
                                  out=WV[pf], in_=wv[pf])
                for ic in range(NIC):
                    sl = ic % 4
                    if ic + 3 < NIC:
                        if ic >= 1:
                            W("sync", sACT, ab_dve)
                            W("sync", sDVE, sg_act)
                        ld_n[(ic + 3) % 4] = OP("sync", "dma_start",
                                                inc=sLD[(ic + 3) % 2], amt=16,
                                                out=WV[(ic + 3) % 4],
                                                in_=wv[ic + 3])
                    W("scalar", sLD[ic % 2], ld_n[sl])
                    if ic == 0:
                        W("scalar", sPE, swv_free_pe)
                    W("vector", sLD[ic % 2], ld_n[sl])
                    if ic == 0:
                        W("vector", sPE, swv_free_pe)
                    g1 = OP("vector", "tensor_scalar", inc=sDVE, out=SGTMP,
                            in0=WV[sl][:, fb * 512:(fb + 1) * 512], scalar1=0.0,
                            scalar2=None, op0=ALU.is_lt)
                    W("vector", sDVE, g1)
                    sg_act = OP("vector", "scalar_tensor_tensor", inc=sDVE,
                                out=SWV[:, ic * 512:(ic + 1) * 512],
                                in0=WV[sl][:, fb * 512:(fb + 1) * 512],
                                scalar=0.0, in1=SGTMP, op0=ALU.is_gt,
                                op1=ALU.subtract)
                    if ic > 0:
                        W("scalar", sPE, smm)
                    OP("scalar", "activation", inc=sACT, out=ABv,
                       in_=WV[sl][:, fb * 512:(fb + 1) * 512],
                       func=AF.Abs)
                    ab_dve = actfence()
                    W("tensor", sACT, ab_dve)
                    smm = mm(P[0:1, 2048:2560], ONESF[:, :], ABv,
                             start=(ic == 0), stop=(ic == NIC - 1), inc=sPE)
                for _d in range(8):
                    sg_act = OP("vector", "tensor_copy", inc=sDVE,
                                out=SGTMP.bitcast(BF16)[:, 0:512],
                                in_=SWV[:, 31 * 512:32 * 512])
                rs = ROWS[0:1, 0:512]
                W("vector", sPE, smm)
                d1 = OP("vector", "tensor_scalar", inc=sDVE, out=rs,
                        in0=P[0:1, 2048:2560], scalar1=INV_D, scalar2=1e-8,
                        op0=ALU.mult, op1=ALU.max)
                W("vector", sDVE, d1)
                W("vector", sLDA, vsn)
                d2 = OP("vector", "tensor_tensor", inc=sDVE, out=rs, in0=rs,
                        in1=VSl[0:1, fb * 512:(fb + 1) * 512], op=ALU.mult)
                W("tensor", sDVE, d2)
                bc = mm(P[:, 3072:3584], ONES1[:, :], rs, start=True, stop=True,
                        inc=sPE)
                W("vector", sPE, bc)
                cp = OP("vector", "tensor_copy", inc=sDVE, out=svb,
                        in_=P[:, 3072:3584])
                if swv_d is not None and swv_i[0] < 4:
                    W("gpsimd", sDVE, sg_act)
                    OP("gpsimd", "dma_start", inc=sST, amt=16,
                       out=swv_d[swv_i[0]], in_=SWV[:, :])
                    swv_i[0] += 1
                evac = {}
                stn = {}
                pe_n = 0
                for tc in range(16):
                    bank = P[:, (tc % 2) * 512:(tc % 2) * 512 + 512]
                    W("tensor", sDVE, sg_act)
                    if tc >= 2 and (tc - 2) in evac:
                        W("tensor", sDVE, evac[tc - 2])
                    for ic in range(NIC):
                        pe_n = mm(bank, xncol(ic, tc * 128, 128),
                                  SWV[:, ic * 512:(ic + 1) * 512],
                                  start=(ic == 0), stop=(ic == NIC - 1),
                                  inc=(sPE if ic == NIC - 1 else None))
                    W("vector", sPE, pe_n)
                    W("vector", sDVE, cp)
                    if tc >= 2 and (tc - 2) in stn:
                        W("vector", sST, stn[tc - 2])
                    evac[tc] = OP("vector", "tensor_tensor", inc=sDVE,
                                  out=STv[tc % 2], in0=bank, in1=svb, op=ALU.mult)
                    W("gpsimd", sDVE, evac[tc])
                    for j in range(4):
                        stn[tc] = OP("gpsimd", "dma_start", inc=sST, amt=16,
                                     out=v_s[fb * 4 + j, half * 16 + tc, :, :],
                                     in_=STv[tc % 2].rearrange(
                                         "p (j d) -> p j d", j=4)[:, j, :])

        def stage1():
            if upto < 1:
                for tti in range(NTT):
                    def load_fn(tti=tti):
                        c0 = tti * 512
                        src2 = x_oth[:, c0:c0 + 512].rearrange("(c p) t -> p c t",
                                                               p=128)
                        OP("sync", "dma_start", inc=sXT, amt=16,
                           out=A2f[:, 0:8192].rearrange("p (c t) -> p c t", c=16),
                           in_=src2[:, 0:16, :])
                        return OP("sync", "dma_start", inc=sXT, amt=16,
                                  out=A2f[:, 8192:16384].rearrange(
                                      "p (c t) -> p c t", c=16),
                                  in_=src2[:, 16:32, :])
                    W("sync", sDVE, sDVE.n)
                    W("sync", sPE, sPE.n)
                    ln_tile(load_fn, tti * 512)
                if xn_d is not None:
                    W("gpsimd", sDVE, sDVE.n)
                    for ic in range(NIC):
                        OP("gpsimd", "dma_start", inc=sST, amt=16, out=xn_d[ic],
                           in_=xncol(ic, 0, TOWN))
                barrier()
                return
            for half, xsrc in ((1, x_oth), (0, x_own)):
                for tti in range(NTT):
                    def load_fn(tti=tti, xsrc=xsrc):
                        c0 = tti * 512
                        src = xsrc[:, c0:c0 + 512].rearrange("(c p) t -> p c t",
                                                             p=128)
                        OP("sync", "dma_start", inc=sXT, amt=16,
                           out=A2f[:, 0:8192].rearrange("p (c t) -> p c t", c=16),
                           in_=src[:, 0:16, :])
                        return OP("sync", "dma_start", inc=sXT, amt=16,
                                  out=A2f[:, 8192:16384].rearrange(
                                      "p (c t) -> p c t", c=16),
                                  in_=src[:, 16:32, :])
                    W("sync", sDVE, sDVE.n)
                    W("sync", sPE, sPE.n)
                    ln_tile(load_fn, tti * 512)
                barrier()

                def k_store(oc, tt, src, half=half):
                    c0 = half * TOWN + tt * TT
                    return OP("gpsimd", "dma_start", inc=sST, amt=16,
                              out=k_s[oc, :, c0:c0 + TT], in_=src)
                proj_fm(wk, NKV, KS, k_store)
                if half == 0:
                    def q_store(oc, tt, src):
                        return OP("gpsimd", "dma_start", inc=sST, amt=16,
                                  out=q_s[oc, :, tt * TT:(tt + 1) * TT], in_=src)
                    proj_fm(wq, NH, QS, q_store)
                barrier()
                stage_v(half)
                barrier()
            if xn_d is not None:
                for ic in range(NIC):
                    OP("gpsimd", "dma_start", inc=sST, amt=16, out=xn_d[ic],
                       in_=xncol(ic, 0, TOWN))
                barrier()

        def stage2():
            Kt = A2b[:, 0:4096]
            Vt = A2b[:, 4096:8192]
            QSL = [A2b[:, 8192:10240], A2b[:, 10240:12288]]
            EX = A2b[:, 12288:28672]
            SQt = A2f[:, 14336:16384]
            rdb = A2f[:, 16384:16896]
            kv_n = 0
            q_n = [0, 0]
            prev_tail = 0
            prev_xnw = 0
            prev_rdbcp = 0
            prev_recip = 0
            prev_exp_last = 0

            def running_max(dst_cell, tmp_cell, first):
                if first:
                    return OP("vector", "tensor_reduce", inc=sDVE, out=dst_cell,
                              in_=P[0:1, 3584:4096], axis=AX.X, op=ALU.max)
                r1 = OP("vector", "tensor_reduce", inc=sDVE, out=tmp_cell,
                        in_=P[0:1, 3584:4096], axis=AX.X, op=ALU.max)
                W("vector", sDVE, r1)
                return OP("vector", "tensor_tensor", inc=sDVE, out=dst_cell,
                          in0=dst_cell, in1=tmp_cell, op=ALU.max)

            for kvh in range(NKV):
                W("sync", sPE, prev_tail)
                OP("sync", "dma_start", inc=sKV, amt=16, out=Kt, in_=k_s[kvh])
                kv_n = OP("sync", "dma_start", inc=sKV, amt=16,
                          out=Vt.rearrange("p (c d) -> p c d", c=NIC),
                          in_=v_s[kvh].rearrange("c p d -> p c d"))
                W("vector", sKV, kv_n)
                for _d in range(8):
                    kv_slack = OP("vector", "tensor_copy", inc=sDVE,
                                  out=SQt[:, 0:128], in_=SQt[:, 128:256])
                W("tensor", sACT, sACT.n)
                km = 0
                for hh in range(2):
                    W("vector", sKV, kv_n)
                    if hh == 1:
                        W("vector", sPE, sPE.n)
                    km = OP("vector", "tensor_tensor", inc=sDVE, out=SQt,
                            in0=Kt[:, hh * 2048:(hh + 1) * 2048],
                            in1=Kt[:, hh * 2048:(hh + 1) * 2048], op=ALU.mult)
                    for j in range(4):
                        W("tensor", sDVE, sDVE.n)
                        pn = mm(P[0:1, 3584:4096], ONESF[:, :],
                                SQt[:, j * 512:(j + 1) * 512], start=True,
                                stop=True, inc=sPE)
                        W("vector", sPE, pn)
                        running_max(ROWS[0:1, 0:1], ROWS[0:1, 1:2],
                                    first=(hh == 0 and j == 0))
                kmax_dve = sDVE.n
                for qi in range(4):
                    qh = kvh * 4 + qi
                    sl = qh % 2
                    W("sync", sPE, prev_tail)
                    q_n[sl] = OP("sync", "dma_start", inc=sQL, amt=16, out=QSL[sl],
                                 in_=q_s[qh])
                    W("vector", sQL, q_n[sl])
                    for _d in range(6):
                        OP("vector", "tensor_copy", inc=sDVE,
                           out=SQt[:, 0:128], in_=SQt[:, 128:256])
                    W("tensor", sACT, sACT.n)
                    W("vector", sQL, q_n[sl])
                    W("vector", sPE, sPE.n)
                    qm = OP("vector", "tensor_tensor", inc=sDVE, out=SQt[:, 0:2048],
                            in0=QSL[sl], in1=QSL[sl], op=ALU.mult)
                    for j in range(4):
                        W("tensor", sDVE, sDVE.n)
                        pn = mm(P[0:1, 3584:4096], ONESF[:, :],
                                SQt[:, j * 512:(j + 1) * 512], start=True,
                                stop=True, inc=sPE)
                        W("vector", sPE, pn)
                        running_max(ROWS[0:1, 3:4], ROWS[0:1, 2:3], first=(j == 0))
                    W("vector", sDVE, kmax_dve)
                    t1 = OP("vector", "tensor_tensor", inc=sDVE, out=ROWS[0:1, 4:5],
                            in0=ROWS[0:1, 3:4], in1=ROWS[0:1, 0:1], op=ALU.mult)
                    W("scalar", sDVE, t1)
                    aa = OP("scalar", "activation", inc=sACT, out=ROWS[0:1, 5:6],
                            in_=ROWS[0:1, 4:5], func=AF.Ln)
                    W("scalar", sACT, aa)
                    a2 = OP("scalar", "activation", inc=sACT, out=ROWS[0:1, 6:7],
                            in_=ROWS[0:1, 5:6], func=AF.Exp, scale=0.5)
                    W("vector", sACT, a2)
                    t2 = OP("vector", "tensor_scalar", inc=sDVE, out=ROWS[0:1, 7:8],
                            in0=ROWS[0:1, 6:7], scalar1=-RSQ, scalar2=None,
                            op0=ALU.mult)
                    W("tensor", sDVE, t2)
                    W("tensor", sDVE, prev_rdbcp)
                    bcn = mm(P[:, 3584:3585], ONES1[:, :], ROWS[0:1, 7:8],
                             start=True, stop=True, inc=sPE)
                    W("scalar", sPE, bcn)
                    OP("scalar", "activation", inc=sACT, out=NEGC[:, :],
                       in_=P[:, 3584:3585], func=AF.Copy)
                    negc_act = OP("scalar", "activation", inc=sACT,
                                  out=CONSTT[0:1, 2:3], in_=CONSTT[0:1, 0:1],
                                  func=AF.Copy)
                    for qt in range(NTT):
                        exp_n = [0] * 16
                        st_pe = 0
                        denom_start = True
                        att_start = True
                        for g in range(16):
                            wb = P[:, (g % 2) * 1024:(g % 2) * 1024 + 1024]
                            if g >= 2:
                                W("tensor", sACT, exp_n[g - 2])
                            if g == 0:
                                W("tensor", sKV, kv_n)
                                W("tensor", sQL, q_n[sl])
                                W("tensor", sACT, prev_exp_last)
                            for u in range(2):
                                kc = 2 * g + u
                                st_pe = mm(wb[:, u * 512:(u + 1) * 512],
                                           Kt[:, kc * 128:(kc + 1) * 128],
                                           QSL[sl][:, qt * 512:(qt + 1) * 512],
                                           start=True, stop=True,
                                           inc=(sPE if u == 1 else None),
                                           skip=True)
                            W("scalar", sPE, st_pe)
                            if g == 0:
                                W("scalar", sPE, prev_tail)
                                W("scalar", sACT, negc_act)
                            OP("scalar", "activation", inc=sACT,
                               out=EX[:, g * 1024:(g + 1) * 1024],
                               in_=wb, func=AF.Exp, bias=NEGC[:, :],
                               scale=RSQ)
                            exp_n[g] = actfence()
                            if g >= 1:
                                W("tensor", sACT, exp_n[g - 1])
                                if g == 1:
                                    W("tensor", sDVE, prev_recip)
                                    W("tensor", sDVE, prev_xnw)
                                for u in range(2):
                                    kc = 2 * (g - 1) + u
                                    mm(P[0:1, 2048:2560], ONESB[:, :],
                                       EX[:, kc * 512:(kc + 1) * 512],
                                       start=denom_start, stop=False, skip=True)
                                    denom_start = False
                                    mm(P[:, 3072:3584],
                                       Vt[:, kc * 128:(kc + 1) * 128],
                                       EX[:, kc * 512:(kc + 1) * 512],
                                       start=att_start, stop=False, skip=True)
                                    att_start = False
                        W("tensor", sACT, exp_n[15])
                        tail_pe = 0
                        for u in range(2):
                            kc = 30 + u
                            mm(P[0:1, 2048:2560], ONESB[:, :],
                               EX[:, kc * 512:(kc + 1) * 512],
                               start=False, stop=(u == 1), skip=True)
                            tail_pe = mm(P[:, 3072:3584],
                                         Vt[:, kc * 128:(kc + 1) * 128],
                                         EX[:, kc * 512:(kc + 1) * 512],
                                         start=False, stop=(u == 1),
                                         inc=(sPE if u == 1 else None), skip=True)
                        W("vector", sPE, tail_pe)
                        prev_recip = OP("vector", "reciprocal", inc=sDVE,
                                        out=ROWS[0:1, 512:1024],
                                        in_=P[0:1, 2048:2560])
                        W("tensor", sDVE, prev_recip)
                        W("tensor", sDVE, prev_rdbcp)
                        rb_pe = mm(P[:, 2560:3072], ONES1[:, :],
                                   ROWS[0:1, 512:1024], start=True, stop=True,
                                   inc=sPE)
                        W("vector", sPE, rb_pe)
                        prev_rdbcp = OP("vector", "tensor_copy", inc=sDVE, out=rdb,
                                        in_=P[:, 2560:3072])
                        W("vector", sDVE, prev_rdbcp)
                        prev_xnw = OP("vector", "tensor_tensor", inc=sDVE,
                                      out=xncol(qh, qt * TT, TT),
                                      in0=P[:, 3072:3584], in1=rdb, op=ALU.mult)
                        prev_tail = tail_pe
                        prev_exp_last = exp_n[15]
            if xn2_d is not None:
                W("gpsimd", sDVE, sDVE.n)
                W("gpsimd", sPE, sPE.n)
                for ic in range(NIC):
                    OP("gpsimd", "dma_start", inc=sST, amt=16, out=xn2_d[ic],
                       in_=xncol(ic, 0, TOWN))
            barrier()

        def stage3():
            WA = [A2f[:, 0:4096], A2f[:, 4096:8192]]
            AB = A2f[:, 8192:9216]
            scol = A2f[:, 9216:9217]
            scol2 = A2f[:, 9217:9218]
            SG = A2b[:, 20480:24576]
            OST = [A2f[:, 12288 + i * 512: 12288 + (i + 1) * 512] for i in range(2)]
            O2 = A2f[:, 13312:13824]
            RSUM = [A2f[0:1, 13824 + t * 512: 13824 + (t + 1) * 512]
                    for t in range(4)]
            RSQS = [A2f[0:1, 15872 + t * 512: 15872 + (t + 1) * 512]
                    for t in range(3)] + [A2f[0:1, 9728:10240]]
            ld_n = [0, 0]
            sign_act = 0
            abs_dve = 0
            proj_pe = {}
            evac = {}
            for oc in range(NH):
                sl = oc % 2
                if oc >= 2:
                    W("sync", sACT, max(sign_act, abs_dve))
                ld_n[sl] = OP("sync", "dma_start", inc=sLD[sl], amt=16,
                              out=WA[sl].rearrange("p (c k) -> p c k", c=NIC),
                              in_=wo[oc])
                W("scalar", sLD[sl], ld_n[sl])
                if oc >= 1:
                    W("scalar", sPE, proj_pe.get(oc - 1, 0))
                sign_act = OP("scalar", "activation", inc=sACT, out=SG[:, :],
                              in_=WA[sl][:, :], func=AF.Sign)
                qtr_pe = 0
                for qtr in range(4):
                    W("scalar", sPE, qtr_pe)
                    OP("scalar", "activation", inc=sACT, out=AB[:, :],
                       in_=WA[sl][:, qtr * 1024:(qtr + 1) * 1024],
                       func=AF.Abs)
                    abs_act = actfence()
                    W("tensor", sACT, abs_act)
                    for j in range(8):
                        ic = qtr * 8 + j
                        qtr_pe = mm(P[:, 2048:2049], AB[:, j * 128:(j + 1) * 128],
                                    ONESF[:, :], start=(ic == 0),
                                    stop=(ic == NIC - 1),
                                    inc=(sPE if j == 7 else None))
                abs_dve = abs_act
                sign_act = abs_act
                W("vector", sPE, sPE.n)
                d1 = OP("vector", "tensor_scalar", inc=sDVE, out=scol,
                        in0=P[:, 2048:2049], scalar1=INV_D, scalar2=1e-8,
                        op0=ALU.mult, op1=ALU.max)
                W("vector", sDVE, d1)
                d2 = OP("vector", "tensor_tensor", inc=sDVE, out=scol2, in0=scol,
                        in1=OS[:, oc:oc + 1], op=ALU.mult)
                for tt in range(NTT):
                    bank = P[:, (tt % 2) * 512:(tt % 2) * 512 + 512]
                    W("tensor", sACT, sign_act)
                    pk = (oc, tt - 2) if tt >= 2 else \
                        ((oc - 1, NTT - 2 + tt) if oc > 0 else None)
                    if pk is not None and pk in evac:
                        W("tensor", sDVE, evac[pk])
                    for ic in range(NIC):
                        proj_pe[oc] = mm(bank, SG[:, ic * 128:(ic + 1) * 128],
                                         xncol(ic, tt * TT, TT), start=(ic == 0),
                                         stop=(ic == NIC - 1),
                                         inc=(sPE if ic == NIC - 1 else None))
                    W("vector", sPE, proj_pe[oc])
                    W("vector", sDVE, d2)
                    if pk is not None:
                        W("vector", sST, sST.n)
                    osl = OST[tt % 2]
                    evac[(oc, tt)] = OP("vector", "tensor_scalar", inc=sDVE,
                                        out=osl, in0=bank, scalar1=scol2,
                                        scalar2=None, op0=ALU.mult)
                    W("gpsimd", sDVE, evac[(oc, tt)])
                    OP("gpsimd", "dma_start", inc=sST, amt=16, out=o_s[tt, oc], in_=osl)
                    W("vector", sDVE, evac[(oc, tt)])
                    o2d = OP("vector", "tensor_tensor", inc=sDVE, out=O2, in0=osl,
                             in1=osl, op=ALU.mult)
                    W("tensor", sDVE, o2d)
                    mm(P[0:1, 2560:3072], ONESF[:, :], osl, start=True, stop=True)
                    p2 = mm(P[0:1, 3072:3584], ONESF[:, :], O2, start=True,
                            stop=True, inc=sPE)
                    W("vector", sPE, p2)
                    if oc == 0:
                        a1 = OP("vector", "tensor_copy", inc=sDVE, out=RSUM[tt],
                                in_=P[0:1, 2560:3072])
                        W("vector", sDVE, a1)
                        OP("vector", "tensor_copy", inc=sDVE, out=RSQS[tt],
                           in_=P[0:1, 3072:3584])
                    else:
                        a1 = OP("vector", "tensor_tensor", inc=sDVE, out=RSUM[tt],
                                in0=RSUM[tt], in1=P[0:1, 2560:3072], op=ALU.add)
                        W("vector", sDVE, a1)
                        OP("vector", "tensor_tensor", inc=sDVE, out=RSQS[tt],
                           in0=RSQS[tt], in1=P[0:1, 3072:3584], op=ALU.add)
            barrier()
            ln_apply_residual(o_s, x_own, x2_s, RSUM, RSQS, store_out=False)
            barrier()

        def ln_apply_residual(src_s, base2d, dst_s, RSUM, RSQS, store_out):
            TLD = [A2f[:, i * 512:(i + 1) * 512] for i in range(4)]
            BLD = [A2f[:, 2048 + i * 512: 2048 + (i + 1) * 512] for i in range(4)]
            TMP = A2f[:, 4096:4608]
            OUTT = [A2f[:, 4608 + i * 512: 4608 + (i + 1) * 512] for i in range(2)]
            for tt in range(NTT):
                rm = ROWS[0:1, 0:512]
                rv = ROWS[0:1, 512:1024]
                rt = ROWS[0:1, 1024:1536]
                rvar = ROWS[0:1, 1536:2048]
                rln = ROWS[0:1, 512:1024]
                rstd = ROWS[0:1, 1024:1536]
                rt2 = ROWS[0:1, 1536:2048]
                W("vector", sDVE, sDVE.n)
                c1 = OP("vector", "tensor_scalar", inc=sDVE, out=rm, in0=RSUM[tt],
                        scalar1=INV_D, scalar2=None, op0=ALU.mult)
                c2 = OP("vector", "tensor_scalar", inc=sDVE, out=rv, in0=RSQS[tt],
                        scalar1=INV_D, scalar2=None, op0=ALU.mult)
                W("vector", sDVE, c2)
                c3 = OP("vector", "tensor_tensor", inc=sDVE, out=rt, in0=rm,
                        in1=rm, op=ALU.mult)
                W("vector", sDVE, c3)
                c4 = OP("vector", "tensor_tensor", inc=sDVE, out=rvar, in0=rv,
                        in1=rt, op=ALU.subtract)
                W("scalar", sDVE, c4)
                aa = OP("scalar", "activation", inc=sACT, out=rln, in_=rvar,
                        func=AF.Ln, bias=EPS1[0:1, 0:1])
                W("scalar", sACT, aa)
                OP("scalar", "activation", inc=sACT, out=rstd, in_=rln,
                   func=AF.Exp, scale=-0.5)
                a2 = actfence()
                W("vector", sACT, a2)
                c5 = OP("vector", "scalar_tensor_tensor", inc=sDVE, out=rt2, in0=rm,
                        scalar=-1.0, in1=rstd, op0=ALU.mult, op1=ALU.mult)
                W("tensor", sDVE, c5)
                mm(P[:, 3072:3584], ONES1[:, :], rstd, start=True, stop=True)
                bc = mm(P[:, 3584:4096], ONES1[:, :], rt2, start=True, stop=True,
                        inc=sPE)
                ld = [0, 0, 0, 0]
                ev = {}
                stn = {}

                def issue(fc2, tt=tt):
                    sl2 = fc2 % 4
                    OP("sync", "dma_start", inc=sLD[fc2 % 2], amt=16,
                       out=TLD[sl2], in_=src_s[tt, fc2])
                    ld[sl2] = OP("sync", "dma_start", inc=sLD[fc2 % 2], amt=16,
                                 out=BLD[sl2],
                                 in_=base2d[fc2 * 128:(fc2 + 1) * 128,
                                            tt * TT:(tt + 1) * TT])
                W("sync", sDVE, sDVE.n)
                for fc0 in range(3):
                    issue(fc0)
                for fc in range(NIC):
                    sl = fc % 4
                    if fc + 3 < NIC:
                        if fc >= 1 and (fc - 1) in ev:
                            W("sync", sDVE, ev[fc - 1])
                        issue(fc + 3)
                    W("vector", sLD[fc % 2], ld[sl])
                    W("vector", sPE, bc)
                    t1 = OP("vector", "tensor_tensor", inc=sDVE, out=TMP,
                            in0=TLD[sl], in1=P[:, 3072:3584], op=ALU.mult)
                    W("vector", sDVE, t1)
                    t2 = OP("vector", "tensor_tensor", inc=sDVE, out=TMP,
                            in0=TMP, in1=P[:, 3584:4096], op=ALU.add)
                    W("vector", sDVE, t2)
                    if fc >= 2 and (fc - 2) in stn:
                        W("vector", sST, stn[fc - 2])
                    ev[fc] = OP("vector", "tensor_tensor", inc=sDVE,
                                out=OUTT[fc % 2], in0=BLD[sl], in1=TMP, op=ALU.add)
                    W("gpsimd", sDVE, ev[fc])
                    if store_out:
                        stn[fc] = OP("gpsimd", "dma_start", inc=sST, amt=16,
                                     out=out_t[fc * 128:(fc + 1) * 128,
                                               tt * TT:(tt + 1) * TT],
                                     in_=OUTT[fc % 2])
                    else:
                        stn[fc] = OP("gpsimd", "dma_start", inc=sST, amt=16,
                                     out=dst_s[tt, fc], in_=OUTT[fc % 2])

        def stage4a():
            for tt in range(NTT):
                def load_fn(tt=tt):
                    OP("sync", "dma_start", inc=sXT, amt=16,
                       out=A2f[:, 0:8192].rearrange("p (c t) -> p c t", c=16),
                       in_=x2_s[tt, 0:16].rearrange("c p t -> p c t"))
                    return OP("sync", "dma_start", inc=sXT, amt=16,
                              out=A2f[:, 8192:16384].rearrange(
                                  "p (c t) -> p c t", c=16),
                              in_=x2_s[tt, 16:32].rearrange("c p t -> p c t"))
                W("sync", sDVE, sDVE.n)
                W("sync", sPE, sPE.n)
                W("sync", sST, sST.n)
                ln_tile(load_fn, tt * 512)
            barrier()

        def stage4b():
            WG = [A2b[:, 0:4096], A2b[:, 4096:8192]]
            WU = [A2b[:, 8192:12288], A2b[:, 12288:16384]]
            SGT = A2b[:, 16384:16896]
            STH = [A2b[:, 16896 + i * 512: 16896 + (i + 1) * 512] for i in range(2)]
            gld = [0, 0]
            uld = [0, 0]
            pe_n = {}
            hmul = {}
            stn = {}
            sil = {}
            for oc in range(NFF):
                sl = oc % 2
                if oc >= 2:
                    W("gpsimd", sPE, pe_n.get((oc - 2, NTT - 1), 0))
                gld[sl] = OP("gpsimd", "dma_start", inc=sGC[sl], amt=16,
                             out=WG[sl].rearrange("p (c k) -> p c k", c=NIC),
                             in_=wg[oc])
                uld[sl] = OP("gpsimd", "dma_start", inc=sGC[sl], amt=16,
                             out=WU[sl].rearrange("p (c k) -> p c k", c=NIC),
                             in_=wu[oc])
                for tt in range(NTT):
                    gb = P[:, (tt % 2) * 512:(tt % 2) * 512 + 512]
                    ub = P[:, 1024 + (tt % 2) * 512:1024 + (tt % 2) * 512 + 512]
                    W("tensor", sGC[sl], uld[sl])
                    pk = (oc, tt - 2) if tt >= 2 else \
                        ((oc - 1, NTT - 2 + tt) if oc > 0 else None)
                    if pk is not None:
                        if pk in sil:
                            W("tensor", sACT, sil[pk])
                        if pk in hmul:
                            W("tensor", sDVE, hmul[pk])
                    for ic in range(NIC):
                        mm(gb, WG[sl][:, ic * 128:(ic + 1) * 128],
                           xncol(ic, tt * TT, TT), start=(ic == 0),
                           stop=(ic == NIC - 1))
                    for ic in range(NIC):
                        pe_n[(oc, tt)] = mm(ub, WU[sl][:, ic * 128:(ic + 1) * 128],
                                            xncol(ic, tt * TT, TT),
                                            start=(ic == 0), stop=(ic == NIC - 1),
                                            inc=(sPE if ic == NIC - 1 else None))
                    W("scalar", sPE, pe_n[(oc, tt)])
                    if pk is not None and pk in hmul:
                        W("scalar", sDVE, hmul[pk])
                    OP("scalar", "activation", inc=sACT, out=SGT,
                       in_=gb, func=AF.Silu)
                    sil[(oc, tt)] = OP("scalar", "activation", inc=sACT,
                                       out=CONSTT[0:1, 2:3],
                                       in_=CONSTT[0:1, 0:1], func=AF.Copy)
                    W("vector", sACT, sil[(oc, tt)])
                    if pk is not None and pk in stn:
                        W("vector", sST, stn[pk])
                    hmul[(oc, tt)] = OP("vector", "tensor_tensor", inc=sDVE,
                                        out=STH[tt % 2], in0=SGT, in1=ub,
                                        op=ALU.mult)
                    W("gpsimd", sDVE, hmul[(oc, tt)])
                    stn[(oc, tt)] = OP("gpsimd", "dma_start", inc=sST, amt=16,
                                       out=h_s[tt, :, oc, :], in_=STH[tt % 2])
            barrier()

        def stage4c():
            WD = [A2b[:, 0:11008], A2b[:, 11008:22016]]
            YST = [A2f[:, 11008 + i * 512: 11008 + (i + 1) * 512] for i in range(2)]
            Y2 = A2f[:, 12032:12544]
            HT = XNa[:, 0:NFF * TT]
            for tt in range(NTT):
                W("sync", sST, sST.n)
                W("sync", sPE, sPE.n)
                hln = OP("sync", "dma_start", inc=sXT, amt=16,
                         out=HT.rearrange("p (c t) -> p c t", c=NFF),
                         in_=h_s[tt])
                wld = [0, 0]
                pe_n = {}
                ev = {}
                for oc in range(NIC):
                    sl = oc % 2
                    if oc >= 2:
                        W("gpsimd", sPE, pe_n.get(oc - 2, 0))
                    wld[sl] = OP("gpsimd", "dma_start", inc=sGC[sl], amt=16,
                                 out=WD[sl].rearrange("p (c k) -> p c k", c=NFF),
                                 in_=wd[oc])
                    yb = P[:, (oc % 2) * 512:(oc % 2) * 512 + 512]
                    W("tensor", sGC[sl], wld[sl])
                    if oc == 0:
                        W("tensor", sXT, hln)
                    if oc >= 2 and (oc - 2) in ev:
                        W("tensor", sDVE, ev[oc - 2])
                    for ic in range(NFF):
                        pe_n[oc] = mm(yb, WD[sl][:, ic * 128:(ic + 1) * 128],
                                      HT[:, ic * TT:(ic + 1) * TT],
                                      start=(ic == 0), stop=(ic == NFF - 1),
                                      inc=(sPE if ic == NFF - 1 else None))
                    W("vector", sPE, pe_n[oc])
                    if oc >= 2:
                        W("vector", sST, sST.n)
                    ysl = YST[oc % 2]
                    ev[oc] = OP("vector", "tensor_copy", inc=sDVE, out=ysl, in_=yb)
                    W("gpsimd", sDVE, ev[oc])
                    OP("gpsimd", "dma_start", inc=sST, amt=16, out=y_s[tt, oc],
                       in_=ysl)
                    W("vector", sDVE, ev[oc])
                    y2d = OP("vector", "tensor_tensor", inc=sDVE, out=Y2, in0=ysl,
                             in1=ysl, op=ALU.mult)
                    W("tensor", sDVE, y2d)
                    mm(P[0:1, 2048:2560], ONESF[:, :], ysl,
                       start=(oc == 0), stop=(oc == NIC - 1), skip=True)
                    mm(P[0:1, 2560:3072], ONESF[:, :], Y2,
                       start=(oc == 0), stop=(oc == NIC - 1),
                       inc=(sPE if oc == NIC - 1 else None), skip=True)
                stats_pe = sPE.n
                W("vector", sPE, stats_pe)
                cs = OP("vector", "tensor_copy", inc=sDVE, out=ROWS[0:1, 0:512],
                        in_=P[0:1, 2048:2560])
                W("vector", sDVE, cs)
                OP("vector", "tensor_copy", inc=sDVE, out=ROWS[0:1, 512:1024],
                   in_=P[0:1, 2560:3072])
                ln_apply_one_tt(tt)
            barrier()

        def ln_apply_one_tt(tt):
            rm = ROWS[0:1, 0:512]
            rv = ROWS[0:1, 512:1024]
            rt = ROWS[0:1, 1024:1536]
            rvar = ROWS[0:1, 1536:2048]
            rln = ROWS[0:1, 512:1024]
            rstd = ROWS[0:1, 1024:1536]
            rt2 = ROWS[0:1, 1536:2048]
            W("vector", sDVE, sDVE.n)
            c1 = OP("vector", "tensor_scalar", inc=sDVE, out=rm, in0=rm,
                    scalar1=INV_D, scalar2=None, op0=ALU.mult)
            c2 = OP("vector", "tensor_scalar", inc=sDVE, out=rv, in0=rv,
                    scalar1=INV_D, scalar2=None, op0=ALU.mult)
            W("vector", sDVE, c2)
            c3 = OP("vector", "tensor_tensor", inc=sDVE, out=rt, in0=rm, in1=rm,
                    op=ALU.mult)
            W("vector", sDVE, c3)
            c4 = OP("vector", "tensor_tensor", inc=sDVE, out=rvar, in0=rv, in1=rt,
                    op=ALU.subtract)
            W("scalar", sDVE, c4)
            aa = OP("scalar", "activation", inc=sACT, out=rln, in_=rvar, func=AF.Ln,
                    bias=EPS1[0:1, 0:1])
            W("scalar", sACT, aa)
            OP("scalar", "activation", inc=sACT, out=rstd, in_=rln, func=AF.Exp,
               scale=-0.5)
            a2 = actfence()
            W("vector", sACT, a2)
            c5 = OP("vector", "scalar_tensor_tensor", inc=sDVE, out=rt2, in0=rm,
                    scalar=-1.0, in1=rstd, op0=ALU.mult, op1=ALU.mult)
            W("tensor", sDVE, c5)
            mm(P[:, 3072:3584], ONES1[:, :], rstd, start=True, stop=True)
            bc = mm(P[:, 3584:4096], ONES1[:, :], rt2, start=True, stop=True,
                    inc=sPE)
            TLD = [A2f[:, 12544 + i * 512: 12544 + (i + 1) * 512] for i in range(4)]
            BLD = [A2f[:, 14592 + i * 512: 14592 + (i + 1) * 512] for i in range(4)]
            TMP = A2f[:, 12032:12544]
            OUTT = [A2f[:, 11008 + i * 512: 11008 + (i + 1) * 512] for i in range(2)]
            ld = [0, 0, 0, 0]
            ev = {}
            stn = {}
            W("sync", sST, sST.n)

            def issue5(fc2, tt=tt):
                sl2 = fc2 % 4
                OP("sync", "dma_start", inc=sLD[fc2 % 2], amt=16,
                   out=TLD[sl2], in_=y_s[tt, fc2])
                ld[sl2] = OP("sync", "dma_start", inc=sLD[fc2 % 2], amt=16,
                             out=BLD[sl2], in_=x2_s[tt, fc2])
            W("sync", sDVE, sDVE.n)
            for fc0 in range(3):
                issue5(fc0)
            for fc in range(NIC):
                sl = fc % 4
                if fc + 3 < NIC:
                    if fc >= 1 and (fc - 1) in ev:
                        W("sync", sDVE, ev[fc - 1])
                    issue5(fc + 3)
                W("vector", sLD[fc % 2], ld[sl])
                W("vector", sPE, bc)
                t1 = OP("vector", "tensor_tensor", inc=sDVE, out=TMP,
                        in0=TLD[sl], in1=P[:, 3072:3584], op=ALU.mult)
                W("vector", sDVE, t1)
                t2 = OP("vector", "tensor_tensor", inc=sDVE, out=TMP,
                        in0=TMP, in1=P[:, 3584:4096], op=ALU.add)
                W("vector", sDVE, t2)
                if fc >= 2 and (fc - 2) in stn:
                    W("vector", sST, stn[fc - 2])
                ev[fc] = OP("vector", "tensor_tensor", inc=sDVE,
                            out=OUTT[fc % 2], in0=BLD[sl], in1=TMP, op=ALU.add)
                W("gpsimd", sDVE, ev[fc])
                stn[fc] = OP("gpsimd", "dma_start", inc=sST, amt=16,
                             out=out_t[fc * 128:(fc + 1) * 128,
                                       tt * TT:(tt + 1) * TT], in_=OUTT[fc % 2])

        if upto >= 0.5:
            stage1()
        if upto >= 2:
            stage2()
        if upto >= 3:
            stage3()
        if upto >= 4:
            stage4a()
            stage4b()
            stage4c()
        for eng in ENGS:
            W(eng, sST, sST.n)
        barrier()

        block = stk.enter_context(nc.Block())

        @block.sync
        def _(e):
            for f in REC["sync"]:
                f(e)

        @block.gpsimd
        def _(e):
            for f in REC["gpsimd"]:
                f(e)

        @block.vector
        def _(e):
            for f in REC["vector"]:
                f(e)

        @block.scalar
        def _(e):
            for f in REC["scalar"]:
                f(e)

        @block.tensor
        def _(e):
            for f in REC["tensor"]:
                f(e)

    return nc


def _layout_inputs(inputs):
    x = np.asarray(inputs["L0_attn_hidden_states"])
    Wq = np.asarray(inputs["L0_attn_q_weight"])
    Wk = np.asarray(inputs["L0_attn_k_weight"])
    Wv = np.asarray(inputs["L0_attn_v_weight"])
    Wo = np.asarray(inputs["L0_attn_o_weight"])
    Wg = np.asarray(inputs["L0_ffn_W_gate"])
    Wu = np.asarray(inputs["L0_ffn_W_up"])
    Wd = np.asarray(inputs["L0_ffn_W_down"])

    def lhst_tiles(Wm, noc, nic):
        return np.ascontiguousarray(
            Wm.reshape(noc, 128, nic, 128).transpose(0, 3, 2, 1))

    shared = {
        "wq": lhst_tiles(Wq, NH, NIC),
        "wk": lhst_tiles(Wk, NKV, NIC),
        "wv": np.ascontiguousarray(Wv.T).reshape(NIC, 128, 1024),
        "wo": lhst_tiles(Wo, NH, NIC),
        "wg": lhst_tiles(Wg, NFF, NIC),
        "wu": lhst_tiles(Wu, NFF, NIC),
        "wd": np.ascontiguousarray(
            Wd.reshape(NIC, 128, NFF, 128).transpose(0, 3, 2, 1)),
        "qs": np.ascontiguousarray(
            np.asarray(inputs["L0_attn_q_scale"]).reshape(NH, 128).T),
        "ks": np.ascontiguousarray(
            np.asarray(inputs["L0_attn_k_scale"]).reshape(NKV, 128).T),
        "vs": np.asarray(inputs["L0_attn_v_scale"]).reshape(1, 1024).copy(),
        "os": np.ascontiguousarray(
            np.asarray(inputs["L0_attn_o_scale"]).reshape(NH, 128).T),
    }
    in_maps = []
    for c in range(8):
        seq, half = c // 2, c % 2
        fm = np.ascontiguousarray(x[seq].T)
        m = dict(shared)
        m["x_own"] = np.ascontiguousarray(fm[:, half * TOWN:(half + 1) * TOWN])
        m["x_oth"] = np.ascontiguousarray(fm[:, (1 - half) * TOWN:(2 - half) * TOWN])
        in_maps.append(m)
    return in_maps


def kernel(**inputs):
    in_maps = _layout_inputs(inputs)
    nc = build_program()
    res = run_bass_kernel_spmd(nc, in_maps, core_ids=list(range(8)))
    out = np.empty((4, SEQ, D), dtype=np.float32)
    for c in range(8):
        seq, half = c // 2, c % 2
        out[seq, half * TOWN:(half + 1) * TOWN, :] = res.results[c]["out"].T
    return out


# version 02
# speedup vs baseline: 1.0158x; 1.0158x over previous
import numpy as np
from contextlib import ExitStack

import concourse.bass as bass
import concourse.mybir as mybir
import concourse.bass_utils as _bu
from concourse.bass_utils import run_bass_kernel_spmd

if not getattr(_bu, "_ldw_patched", False):
    _orig_run_command = _bu.run_command

    def _run_command_ldw(cmd, *a, **kw):
        if isinstance(cmd, (list, tuple)):
            cmd = ["--enable-ldw-opt=true" if c == "--enable-ldw-opt=false" else c
                   for c in cmd]
        return _orig_run_command(cmd, *a, **kw)

    _bu.run_command = _run_command_ldw
    _bu._ldw_patched = True

F32 = mybir.dt.float32
F32R = mybir.dt.float32r
BF16 = mybir.dt.bfloat16
AF = mybir.ActivationFunctionType
ALU = mybir.AluOpType
AX = mybir.AxisListType

D = 4096
NIC = 32
DFF = 11008
NFF = 86
SEQ = 4096
TOWN = 2048
HD = 128
NH = 32
NKV = 8
TT = 512
NTT = 4
RSQ = 0.08838834764831845
INV_D = 1.0 / D
EPS = 1e-5

ENGS = ("sync", "gpsimd", "vector", "scalar", "tensor")


class Sem:
    __slots__ = ("h", "n", "name")

    def __init__(self, h, name):
        self.h = h
        self.n = 0
        self.name = name


def build_program(upto=99, dump=()):
    nc = bass.Bass("TRN2", target_bir_lowering=False)

    def kindof(nm):
        return "ExternalOutput" if nm in dump else "Internal"

    x_own = nc.dram_tensor("x_own", [D, TOWN], F32, kind="ExternalInput").ap()
    x_oth = nc.dram_tensor("x_oth", [D, TOWN], F32, kind="ExternalInput").ap()
    wq = nc.dram_tensor("wq", [NH, 128, NIC, 128], F32, kind="ExternalInput").ap()
    wk = nc.dram_tensor("wk", [NKV, 128, NIC, 128], F32, kind="ExternalInput").ap()
    wv = nc.dram_tensor("wv", [NIC, 128, 1024], F32, kind="ExternalInput").ap()
    wo = nc.dram_tensor("wo", [NH, 128, NIC, 128], F32, kind="ExternalInput").ap()
    wg = nc.dram_tensor("wg", [NFF, 128, NIC, 128], F32, kind="ExternalInput").ap()
    wu = nc.dram_tensor("wu", [NFF, 128, NIC, 128], F32, kind="ExternalInput").ap()
    wd = nc.dram_tensor("wd", [NIC, 128, NFF, 128], F32, kind="ExternalInput").ap()
    qs = nc.dram_tensor("qs", [128, NH], F32, kind="ExternalInput").ap()
    ks = nc.dram_tensor("ks", [128, NKV], F32, kind="ExternalInput").ap()
    vs = nc.dram_tensor("vs", [1, 1024], F32, kind="ExternalInput").ap()
    os_ = nc.dram_tensor("os", [128, NH], F32, kind="ExternalInput").ap()
    out_t = nc.dram_tensor("out", [D, TOWN], F32, kind="ExternalOutput").ap()

    q_s = nc.dram_tensor("q_s", [NH, HD, TOWN], BF16, kind=kindof("q_s")).ap()
    k_s = nc.dram_tensor("k_s", [NKV, HD, SEQ], BF16, kind=kindof("k_s")).ap()
    v_s = nc.dram_tensor("v_s", [NKV, NIC, 128, HD], BF16, kind=kindof("v_s")).ap()
    o_s = nc.dram_tensor("o_s", [NTT, NIC, 128, TT], F32, kind=kindof("o_s")).ap()
    x2_s = nc.dram_tensor("x2_s", [NTT, NIC, 128, TT], F32, kind=kindof("x2_s")).ap()
    h_s = nc.dram_tensor("h_s", [NTT, 128, NFF, TT], BF16, kind=kindof("h_s")).ap()
    y_s = nc.dram_tensor("y_s", [NTT, NIC, 128, TT], F32, kind=kindof("y_s")).ap()
    xn_d = nc.dram_tensor("xn_d", [NIC, 128, TOWN], BF16,
                          kind="ExternalOutput").ap() if "xn_d" in dump else None
    rows_d = nc.dram_tensor("rows_d", [16, 1, 1536], F32,
                            kind="ExternalOutput").ap() if "rows_d" in dump else None
    e2_d = nc.dram_tensor("e2_d", [16, 1, 512], F32,
                          kind="ExternalOutput").ap() if "rows_d" in dump else None
    swv_d = nc.dram_tensor("swv_d", [4, 128, 16384], BF16,
                           kind="ExternalOutput").ap() if "swv_d" in dump else None
    xn2_d = nc.dram_tensor("xn2_d", [NIC, 128, TOWN], BF16,
                           kind="ExternalOutput").ap() if "xn2_d" in dump else None
    swv_i = [0]
    rows_i = [0]

    with ExitStack() as stk:
        e_ = stk.enter_context
        XN = e_(nc.sbuf_tensor("XN", [128, 65536], BF16))
        A2 = e_(nc.sbuf_tensor("A2", [128, 17408], F32))
        ROWS = e_(nc.sbuf_tensor("ROWS", [1, 2048], F32))
        ONESF = e_(nc.sbuf_tensor("ONESF", [128, 1], F32))
        ONESB = e_(nc.sbuf_tensor("ONESB", [128, 1], BF16))
        CONSTT = e_(nc.sbuf_tensor("CONSTT", [128, 32], F32))
        ONES1 = e_(nc.sbuf_tensor("ONES1", [1, 128], F32))
        EPS1 = e_(nc.sbuf_tensor("EPS1", [1, 1], F32))
        NEGC = e_(nc.sbuf_tensor("NEGC", [128, 1], F32))
        QS = e_(nc.sbuf_tensor("QS", [128, NH], F32))
        KS = e_(nc.sbuf_tensor("KS", [128, NKV], F32))
        OS = e_(nc.sbuf_tensor("OS", [128, NH], F32))
        P = e_(nc.psum_tensor("P", [128, 4096], F32))

        A2f = A2[:, :]
        A2r = A2.bitcast(F32R)
        A2b = A2.bitcast(BF16)
        XNa = XN[:, :]

        def xncol(ic, c0, n):
            return XNa[:, ic * TOWN + c0: ic * TOWN + c0 + n]

        sems = {}
        for name in ("sBar", "sPE", "sDVE", "sACT", "sLD0", "sLD1", "sGC0", "sGC1",
                     "sST", "sXT", "sQL", "sKV", "sLDA"):
            sems[name] = Sem(e_(nc.semaphore(name)), name)
        sBar, sPE, sDVE, sACT = (sems["sBar"], sems["sPE"], sems["sDVE"],
                                 sems["sACT"])
        sLD = [sems["sLD0"], sems["sLD1"]]
        sGC = [sems["sGC0"], sems["sGC1"]]
        sST, sXT, sQL, sKV, sLDA = (sems["sST"], sems["sXT"], sems["sQL"],
                                    sems["sKV"], sems["sLDA"])

        REC = {e: [] for e in ENGS}

        def W(eng, sem, n):
            if n > 0:
                REC[eng].append(lambda e, h=sem.h, v=n: e.wait_ge(h, v))

        def OP(eng, meth, inc=None, amt=1, **kw):
            if inc is not None:
                inc.n += amt

            def f(e, meth=meth, kw=kw, inc=inc, amt=amt):
                r = getattr(e, meth)(**kw)
                if inc is not None:
                    r.then_inc(inc.h, amt)
            REC[eng].append(f)
            return inc.n if inc is not None else 0

        def mm(out, lhsT, rhs, start, stop, inc=None, skip=False):
            return OP("tensor", "matmul", inc=inc, out=out, lhsT=lhsT, rhs=rhs,
                      start=start, stop=stop, skip_group_check=skip)

        def actfence():
            return OP("scalar", "activation", inc=sACT, out=CONSTT[0:1, 4:5],
                      in_=CONSTT[0:1, 0:1], func=AF.Copy)

        nbar = [0]

        def barrier():
            W("vector", sDVE, sDVE.n)
            W("scalar", sACT, sACT.n)
            W("tensor", sPE, sPE.n)
            W("gpsimd", sST, sST.n)
            for s in (sGC[0], sGC[1]):
                W("gpsimd", s, s.n)
            for s in (sLD[0], sLD[1], sXT, sKV, sQL, sLDA):
                W("sync", s, s.n)
            nbar[0] += 1
            target = 5 * nbar[0]
            sBar.n = target
            for eng in ENGS:
                REC[eng].append(lambda e, h=sBar.h: e.sem_inc(h, 1))
                REC[eng].append(lambda e, h=sBar.h, v=target: e.wait_ge(h, v))

        c0 = OP("vector", "memset", inc=sDVE, ap=CONSTT[:, :], constant=1.0)
        W("vector", sDVE, c0)
        OP("vector", "tensor_copy", inc=sDVE, out=ONESF[:, :], in_=CONSTT[:, 0:1])
        for _j in range(4):
            OP("vector", "tensor_copy", inc=sDVE, out=ONES1[0:1, _j * 32:(_j + 1) * 32],
               in_=CONSTT[0:1, :])
        OP("vector", "memset", inc=sDVE, ap=ONESB[:, :], constant=1.0)
        OP("vector", "memset", inc=sDVE, ap=EPS1[:, :], constant=EPS)
        OP("sync", "dma_start", inc=sLDA, amt=16, out=QS[:, :], in_=qs)
        OP("sync", "dma_start", inc=sLDA, amt=16, out=KS[:, :], in_=ks)
        OP("sync", "dma_start", inc=sLDA, amt=16, out=OS[:, :], in_=os_)
        consts_dve = sDVE.n
        consts_ld = sLDA.n
        for eng in ("tensor", "vector", "scalar"):
            W(eng, sDVE, consts_dve)
            W(eng, sLDA, consts_ld)
        barrier()

        def ln_tile(load_fn, xn_c0):
            XT = A2f[:, 0:16384]
            ldn = load_fn()
            sq = [A2f[:, 16384 + i * 512: 16384 + (i + 1) * 512] for i in range(2)]
            sq_dve = [0, 0]
            mm_pe = [0, 0]
            for fc in range(NIC):
                sl = fc % 2
                W("vector", sXT, ldn)
                if fc >= 2:
                    W("vector", sPE, mm_pe[sl])
                OP("vector", "tensor_tensor", inc=sDVE, out=sq[sl][:, :],
                   in0=XT[:, fc * 512:(fc + 1) * 512],
                   in1=XT[:, fc * 512:(fc + 1) * 512], op=ALU.mult)
                sq_dve[sl] = OP("vector", "tensor_copy", inc=sDVE,
                                out=CONSTT[0:1, 1:2], in_=CONSTT[0:1, 0:1])
                if fc == 0:
                    W("tensor", sXT, ldn)
                    W("tensor", sDVE, sq_dve[0])
                mm(P[0:1, 2048:2560], ONESF[:, :], XT[:, fc * 512:(fc + 1) * 512],
                   start=(fc == 0), stop=(fc == NIC - 1), skip=True)
                W("tensor", sDVE, sq_dve[sl])
                mm_pe[sl] = mm(P[0:1, 2560:3072], ONESF[:, :], sq[sl][:, :],
                               start=(fc == 0), stop=(fc == NIC - 1), inc=sPE,
                               skip=True)
            stats_done = sPE.n
            rm = ROWS[0:1, 0:512]
            rv = ROWS[0:1, 512:1024]
            rt = ROWS[0:1, 1024:1536]
            rvar = ROWS[0:1, 1536:2048]
            rln = ROWS[0:1, 512:1024]
            rstd = ROWS[0:1, 1024:1536]
            rt2 = ROWS[0:1, 1536:2048]
            W("vector", sPE, stats_done)
            c1 = OP("vector", "tensor_scalar", inc=sDVE, out=rm, in0=P[0:1, 2048:2560],
                    scalar1=INV_D, scalar2=None, op0=ALU.mult)
            c2 = OP("vector", "tensor_scalar", inc=sDVE, out=rv, in0=P[0:1, 2560:3072],
                    scalar1=INV_D, scalar2=None, op0=ALU.mult)
            W("vector", sDVE, c2)
            if e2_d is not None and rows_i[0] < 16:
                W("gpsimd", sDVE, c2)
                dmn = OP("gpsimd", "dma_start", inc=sST, amt=16,
                         out=e2_d[rows_i[0]], in_=ROWS[0:1, 512:1024])
                W("vector", sST, dmn)
            c3 = OP("vector", "tensor_tensor", inc=sDVE, out=rt, in0=rm, in1=rm,
                    op=ALU.mult)
            W("vector", sDVE, c3)
            c4 = OP("vector", "tensor_tensor", inc=sDVE, out=rvar, in0=rv, in1=rt,
                    op=ALU.subtract)
            W("scalar", sDVE, c4)
            aa = OP("scalar", "activation", inc=sACT, out=rln, in_=rvar, func=AF.Ln,
                    bias=EPS1[0:1, 0:1])
            W("scalar", sACT, aa)
            OP("scalar", "activation", inc=sACT, out=rstd, in_=rln, func=AF.Exp,
               scale=-0.5)
            a2 = actfence()
            W("vector", sACT, a2)
            c5 = OP("vector", "scalar_tensor_tensor", inc=sDVE, out=rt2, in0=rm,
                    scalar=-1.0, in1=rstd, op0=ALU.mult, op1=ALU.mult)
            if rows_d is not None and rows_i[0] < 16:
                W("gpsimd", sDVE, c5)
                OP("gpsimd", "dma_start", inc=sST, amt=16,
                   out=rows_d[rows_i[0]], in_=ROWS[0:1, 0:1536])
                rows_i[0] += 1
            W("tensor", sACT, a2)
            W("tensor", sDVE, c5)
            mm(P[:, 3072:3584], ONES1[:, :], rstd, start=True, stop=True)
            bc = mm(P[:, 3584:4096], ONES1[:, :], rt2, start=True, stop=True, inc=sPE)
            TMP = A2f[:, 16384:16896]
            last = 0
            for fc in range(NIC):
                W("vector", sPE, bc)
                if fc > 0:
                    W("vector", sDVE, last)
                t1 = OP("vector", "tensor_tensor", inc=sDVE, out=TMP,
                        in0=XT[:, fc * 512:(fc + 1) * 512], in1=P[:, 3072:3584],
                        op=ALU.mult)
                W("vector", sDVE, t1)
                last = OP("vector", "tensor_tensor", inc=sDVE,
                          out=xncol(fc, xn_c0, 512), in0=TMP, in1=P[:, 3584:4096],
                          op=ALU.add)
            return last

        def proj_fm(wt, noc, scl, store_fn):
            WA = [A2f[:, 0:4096], A2f[:, 4096:8192]]
            AB = A2f[:, 8192:9216]
            scol = A2f[:, 9216:9217]
            scol2 = A2f[:, 9217:9218]
            SG = [A2b[:, 20480:24576], A2b[:, 24576:28672]]
            ST = [A2b[:, 28672 + i * 512: 28672 + (i + 1) * 512] for i in range(2)]
            ld_n = [0, 0]
            sign_act = [0, 0]
            abs_dve = 0
            evac = {}
            store_n = {}
            proj_pe = {}
            for oc in range(noc):
                sl = oc % 2
                if oc >= 2:
                    W("sync", sACT, max(sign_act[sl], abs_dve))
                ld_n[sl] = OP("sync", "dma_start", inc=sLD[sl], amt=16,
                              out=WA[sl].rearrange("p (c k) -> p c k", c=NIC),
                              in_=wt[oc])
                W("scalar", sLD[sl], ld_n[sl])
                if oc >= 2:
                    W("scalar", sPE, proj_pe.get(oc - 2, 0))
                sign_act[sl] = OP("scalar", "activation", inc=sACT, out=SG[sl][:, :],
                                  in_=WA[sl][:, :], func=AF.Sign)
                qtr_pe = 0
                for qtr in range(4):
                    W("scalar", sPE, qtr_pe)
                    OP("scalar", "activation", inc=sACT, out=AB[:, :],
                       in_=WA[sl][:, qtr * 1024:(qtr + 1) * 1024],
                       func=AF.Abs)
                    abs_act = actfence()
                    W("tensor", sACT, abs_act)
                    for j in range(8):
                        ic = qtr * 8 + j
                        qtr_pe = mm(P[:, 2048:2049], AB[:, j * 128:(j + 1) * 128],
                                    ONESF[:, :], start=(ic == 0),
                                    stop=(ic == NIC - 1),
                                    inc=(sPE if j == 7 else None))
                abs_dve = abs_act
                sign_act[sl] = abs_act
                smm_done = sPE.n
                W("vector", sPE, smm_done)
                d1 = OP("vector", "tensor_scalar", inc=sDVE, out=scol,
                        in0=P[:, 2048:2049], scalar1=INV_D, scalar2=1e-8,
                        op0=ALU.mult, op1=ALU.max)
                W("vector", sDVE, d1)
                d2 = OP("vector", "tensor_tensor", inc=sDVE, out=scol2, in0=scol,
                        in1=scl[:, oc:oc + 1], op=ALU.mult)
                for tt in range(NTT):
                    bank = P[:, (tt % 2) * 512:(tt % 2) * 512 + 512]
                    W("tensor", sACT, sign_act[sl])
                    pk = (oc, tt - 2) if tt >= 2 else \
                        ((oc - 1, NTT - 2 + tt) if oc > 0 else None)
                    if pk is not None and pk in evac:
                        W("tensor", sDVE, evac[pk])
                    for ic in range(NIC):
                        proj_pe[oc] = mm(bank, SG[sl][:, ic * 128:(ic + 1) * 128],
                                         xncol(ic, tt * TT, TT), start=(ic == 0),
                                         stop=(ic == NIC - 1),
                                         inc=(sPE if ic == NIC - 1 else None))
                    W("vector", sPE, proj_pe[oc])
                    W("vector", sDVE, d2)
                    if pk is not None and pk in store_n:
                        W("vector", sST, store_n[pk])
                    evac[(oc, tt)] = OP("vector", "tensor_scalar", inc=sDVE,
                                        out=ST[tt % 2], in0=bank, scalar1=scol2,
                                        scalar2=None, op0=ALU.mult)
                    W("gpsimd", sDVE, evac[(oc, tt)])
                    store_n[(oc, tt)] = store_fn(oc, tt, ST[tt % 2])

        def stage_v(half):
            WV = [A2f[:, 8192 + i * 1024: 8192 + (i + 1) * 1024]
                  for i in range(4)]
            SGTMP = A2f[:, 14336:14848]
            ABv = A2f[:, 14848:15360]
            SWV = A2b[:, 0:16384]
            svb = A2f[:, 15360:15872]
            VSl = A2f[0:1, 16384:17408]
            STv = [A2b[:, 31744 + i * 512: 31744 + (i + 1) * 512] for i in range(2)]
            vsn = OP("sync", "dma_start", inc=sLDA, amt=16, out=VSl, in_=vs)
            for fb in range(2):
                ld_n = [0, 0, 0, 0]
                sg_act = 0
                ab_dve = 0
                smm = 0
                swv_free_pe = sPE.n
                for pf in range(3):
                    ld_n[pf] = OP("sync", "dma_start", inc=sLD[pf % 2], amt=16,
                                  out=WV[pf], in_=wv[pf])
                for ic in range(NIC):
                    sl = ic % 4
                    if ic + 3 < NIC:
                        if ic >= 1:
                            W("sync", sACT, ab_dve)
                            W("sync", sDVE, sg_act)
                        ld_n[(ic + 3) % 4] = OP("sync", "dma_start",
                                                inc=sLD[(ic + 3) % 2], amt=16,
                                                out=WV[(ic + 3) % 4],
                                                in_=wv[ic + 3])
                    W("scalar", sLD[ic % 2], ld_n[sl])
                    if ic == 0:
                        W("scalar", sPE, swv_free_pe)
                    W("vector", sLD[ic % 2], ld_n[sl])
                    if ic == 0:
                        W("vector", sPE, swv_free_pe)
                    g1 = OP("vector", "tensor_scalar", inc=sDVE, out=SGTMP,
                            in0=WV[sl][:, fb * 512:(fb + 1) * 512], scalar1=0.0,
                            scalar2=None, op0=ALU.is_lt)
                    W("vector", sDVE, g1)
                    sg_act = OP("vector", "scalar_tensor_tensor", inc=sDVE,
                                out=SWV[:, ic * 512:(ic + 1) * 512],
                                in0=WV[sl][:, fb * 512:(fb + 1) * 512],
                                scalar=0.0, in1=SGTMP, op0=ALU.is_gt,
                                op1=ALU.subtract)
                    if ic > 0:
                        W("scalar", sPE, smm)
                    OP("scalar", "activation", inc=sACT, out=ABv,
                       in_=WV[sl][:, fb * 512:(fb + 1) * 512],
                       func=AF.Abs)
                    ab_dve = actfence()
                    W("tensor", sACT, ab_dve)
                    smm = mm(P[0:1, 2048:2560], ONESF[:, :], ABv,
                             start=(ic == 0), stop=(ic == NIC - 1), inc=sPE)
                for _d in range(8):
                    sg_act = OP("vector", "tensor_copy", inc=sDVE,
                                out=SGTMP.bitcast(BF16)[:, 0:512],
                                in_=SWV[:, 31 * 512:32 * 512])
                rs = ROWS[0:1, 0:512]
                W("vector", sPE, smm)
                d1 = OP("vector", "tensor_scalar", inc=sDVE, out=rs,
                        in0=P[0:1, 2048:2560], scalar1=INV_D, scalar2=1e-8,
                        op0=ALU.mult, op1=ALU.max)
                W("vector", sDVE, d1)
                W("vector", sLDA, vsn)
                d2 = OP("vector", "tensor_tensor", inc=sDVE, out=rs, in0=rs,
                        in1=VSl[0:1, fb * 512:(fb + 1) * 512], op=ALU.mult)
                W("tensor", sDVE, d2)
                bc = mm(P[:, 3072:3584], ONES1[:, :], rs, start=True, stop=True,
                        inc=sPE)
                W("vector", sPE, bc)
                cp = OP("vector", "tensor_copy", inc=sDVE, out=svb,
                        in_=P[:, 3072:3584])
                if swv_d is not None and swv_i[0] < 4:
                    W("gpsimd", sDVE, sg_act)
                    OP("gpsimd", "dma_start", inc=sST, amt=16,
                       out=swv_d[swv_i[0]], in_=SWV[:, :])
                    swv_i[0] += 1
                evac = {}
                stn = {}
                pe_n = 0
                for tc in range(16):
                    bank = P[:, (tc % 2) * 512:(tc % 2) * 512 + 512]
                    W("tensor", sDVE, sg_act)
                    if tc >= 2 and (tc - 2) in evac:
                        W("tensor", sDVE, evac[tc - 2])
                    for ic in range(NIC):
                        pe_n = mm(bank, xncol(ic, tc * 128, 128),
                                  SWV[:, ic * 512:(ic + 1) * 512],
                                  start=(ic == 0), stop=(ic == NIC - 1),
                                  inc=(sPE if ic == NIC - 1 else None))
                    W("vector", sPE, pe_n)
                    W("vector", sDVE, cp)
                    if tc >= 2 and (tc - 2) in stn:
                        W("vector", sST, stn[tc - 2])
                    evac[tc] = OP("vector", "tensor_tensor", inc=sDVE,
                                  out=STv[tc % 2], in0=bank, in1=svb, op=ALU.mult)
                    W("gpsimd", sDVE, evac[tc])
                    for j in range(4):
                        stn[tc] = OP("gpsimd", "dma_start", inc=sST, amt=16,
                                     out=v_s[fb * 4 + j, half * 16 + tc, :, :],
                                     in_=STv[tc % 2].rearrange(
                                         "p (j d) -> p j d", j=4)[:, j, :])

        def stage1():
            if upto < 1:
                for tti in range(NTT):
                    def load_fn(tti=tti):
                        c0 = tti * 512
                        src2 = x_oth[:, c0:c0 + 512].rearrange("(c p) t -> p c t",
                                                               p=128)
                        OP("sync", "dma_start", inc=sXT, amt=16,
                           out=A2f[:, 0:8192].rearrange("p (c t) -> p c t", c=16),
                           in_=src2[:, 0:16, :])
                        return OP("sync", "dma_start", inc=sXT, amt=16,
                                  out=A2f[:, 8192:16384].rearrange(
                                      "p (c t) -> p c t", c=16),
                                  in_=src2[:, 16:32, :])
                    W("sync", sDVE, sDVE.n)
                    W("sync", sPE, sPE.n)
                    ln_tile(load_fn, tti * 512)
                if xn_d is not None:
                    W("gpsimd", sDVE, sDVE.n)
                    for ic in range(NIC):
                        OP("gpsimd", "dma_start", inc=sST, amt=16, out=xn_d[ic],
                           in_=xncol(ic, 0, TOWN))
                barrier()
                return
            for half, xsrc in ((1, x_oth), (0, x_own)):
                for tti in range(NTT):
                    def load_fn(tti=tti, xsrc=xsrc):
                        c0 = tti * 512
                        src = xsrc[:, c0:c0 + 512].rearrange("(c p) t -> p c t",
                                                             p=128)
                        OP("sync", "dma_start", inc=sXT, amt=16,
                           out=A2f[:, 0:8192].rearrange("p (c t) -> p c t", c=16),
                           in_=src[:, 0:16, :])
                        return OP("sync", "dma_start", inc=sXT, amt=16,
                                  out=A2f[:, 8192:16384].rearrange(
                                      "p (c t) -> p c t", c=16),
                                  in_=src[:, 16:32, :])
                    W("sync", sDVE, sDVE.n)
                    W("sync", sPE, sPE.n)
                    ln_tile(load_fn, tti * 512)
                barrier()

                def k_store(oc, tt, src, half=half):
                    c0 = half * TOWN + tt * TT
                    return OP("gpsimd", "dma_start", inc=sST, amt=16,
                              out=k_s[oc, :, c0:c0 + TT], in_=src)
                proj_fm(wk, NKV, KS, k_store)
                if half == 0:
                    def q_store(oc, tt, src):
                        return OP("gpsimd", "dma_start", inc=sST, amt=16,
                                  out=q_s[oc, :, tt * TT:(tt + 1) * TT], in_=src)
                    proj_fm(wq, NH, QS, q_store)
                barrier()
                stage_v(half)
                barrier()
            if xn_d is not None:
                for ic in range(NIC):
                    OP("gpsimd", "dma_start", inc=sST, amt=16, out=xn_d[ic],
                       in_=xncol(ic, 0, TOWN))
                barrier()

        def stage2():
            Kt = A2b[:, 0:4096]
            Vt = A2b[:, 4096:8192]
            QSL = [A2b[:, 8192:10240], A2b[:, 10240:12288]]
            EX = A2b[:, 12288:28672]
            SQt = A2f[:, 14336:16384]
            rdb = A2f[:, 16384:16896]
            kv_n = 0
            q_n = [0, 0]
            prev_tail = 0
            prev_xnw = 0
            prev_rdbcp = 0
            prev_recip = 0
            prev_exp_last = 0

            def running_max(dst_cell, tmp_cell, first):
                if first:
                    return OP("vector", "tensor_reduce", inc=sDVE, out=dst_cell,
                              in_=P[0:1, 3584:4096], axis=AX.X, op=ALU.max)
                r1 = OP("vector", "tensor_reduce", inc=sDVE, out=tmp_cell,
                        in_=P[0:1, 3584:4096], axis=AX.X, op=ALU.max)
                W("vector", sDVE, r1)
                return OP("vector", "tensor_tensor", inc=sDVE, out=dst_cell,
                          in0=dst_cell, in1=tmp_cell, op=ALU.max)

            for kvh in range(NKV):
                W("sync", sPE, prev_tail)
                OP("sync", "dma_start", inc=sKV, amt=16, out=Kt, in_=k_s[kvh])
                kv_n = OP("sync", "dma_start", inc=sKV, amt=16,
                          out=Vt.rearrange("p (c d) -> p c d", c=NIC),
                          in_=v_s[kvh].rearrange("c p d -> p c d"))
                W("vector", sKV, kv_n)
                for _d in range(8):
                    kv_slack = OP("vector", "tensor_copy", inc=sDVE,
                                  out=SQt[:, 0:128], in_=SQt[:, 128:256])
                W("tensor", sACT, sACT.n)
                km = 0
                for hh in range(2):
                    W("vector", sKV, kv_n)
                    if hh == 1:
                        W("vector", sPE, sPE.n)
                    km = OP("vector", "tensor_tensor", inc=sDVE, out=SQt,
                            in0=Kt[:, hh * 2048:(hh + 1) * 2048],
                            in1=Kt[:, hh * 2048:(hh + 1) * 2048], op=ALU.mult)
                    for j in range(4):
                        W("tensor", sDVE, sDVE.n)
                        pn = mm(P[0:1, 3584:4096], ONESF[:, :],
                                SQt[:, j * 512:(j + 1) * 512], start=True,
                                stop=True, inc=sPE)
                        W("vector", sPE, pn)
                        running_max(ROWS[0:1, 0:1], ROWS[0:1, 1:2],
                                    first=(hh == 0 and j == 0))
                kmax_dve = sDVE.n
                for qi in range(4):
                    qh = kvh * 4 + qi
                    sl = qh % 2
                    W("sync", sPE, prev_tail)
                    q_n[sl] = OP("sync", "dma_start", inc=sQL, amt=16, out=QSL[sl],
                                 in_=q_s[qh])
                    W("vector", sQL, q_n[sl])
                    for _d in range(6):
                        OP("vector", "tensor_copy", inc=sDVE,
                           out=SQt[:, 0:128], in_=SQt[:, 128:256])
                    W("tensor", sACT, sACT.n)
                    W("vector", sQL, q_n[sl])
                    W("vector", sPE, sPE.n)
                    qm = OP("vector", "tensor_tensor", inc=sDVE, out=SQt[:, 0:2048],
                            in0=QSL[sl], in1=QSL[sl], op=ALU.mult)
                    for j in range(4):
                        W("tensor", sDVE, sDVE.n)
                        pn = mm(P[0:1, 3584:4096], ONESF[:, :],
                                SQt[:, j * 512:(j + 1) * 512], start=True,
                                stop=True, inc=sPE)
                        W("vector", sPE, pn)
                        running_max(ROWS[0:1, 3:4], ROWS[0:1, 2:3], first=(j == 0))
                    W("vector", sDVE, kmax_dve)
                    t1 = OP("vector", "tensor_tensor", inc=sDVE, out=ROWS[0:1, 4:5],
                            in0=ROWS[0:1, 3:4], in1=ROWS[0:1, 0:1], op=ALU.mult)
                    W("scalar", sDVE, t1)
                    aa = OP("scalar", "activation", inc=sACT, out=ROWS[0:1, 5:6],
                            in_=ROWS[0:1, 4:5], func=AF.Ln)
                    W("scalar", sACT, aa)
                    a2 = OP("scalar", "activation", inc=sACT, out=ROWS[0:1, 6:7],
                            in_=ROWS[0:1, 5:6], func=AF.Exp, scale=0.5)
                    W("vector", sACT, a2)
                    t2 = OP("vector", "tensor_scalar", inc=sDVE, out=ROWS[0:1, 7:8],
                            in0=ROWS[0:1, 6:7], scalar1=-RSQ, scalar2=None,
                            op0=ALU.mult)
                    W("tensor", sDVE, t2)
                    W("tensor", sDVE, prev_rdbcp)
                    bcn = mm(P[:, 3584:3585], ONES1[:, :], ROWS[0:1, 7:8],
                             start=True, stop=True, inc=sPE)
                    W("scalar", sPE, bcn)
                    OP("scalar", "activation", inc=sACT, out=NEGC[:, :],
                       in_=P[:, 3584:3585], func=AF.Copy)
                    negc_act = OP("scalar", "activation", inc=sACT,
                                  out=CONSTT[0:1, 2:3], in_=CONSTT[0:1, 0:1],
                                  func=AF.Copy)
                    for qt in range(NTT):
                        exp_n = [0] * 16
                        st_pe = 0
                        denom_start = True
                        att_start = True
                        for g in range(16):
                            wb = P[:, (g % 2) * 1024:(g % 2) * 1024 + 1024]
                            if g >= 2:
                                W("tensor", sACT, exp_n[g - 2])
                            if g == 0:
                                W("tensor", sKV, kv_n)
                                W("tensor", sQL, q_n[sl])
                                W("tensor", sACT, prev_exp_last)
                            for u in range(2):
                                kc = 2 * g + u
                                st_pe = mm(wb[:, u * 512:(u + 1) * 512],
                                           Kt[:, kc * 128:(kc + 1) * 128],
                                           QSL[sl][:, qt * 512:(qt + 1) * 512],
                                           start=True, stop=True,
                                           inc=(sPE if u == 1 else None),
                                           skip=True)
                            W("scalar", sPE, st_pe)
                            if g == 0:
                                W("scalar", sPE, prev_tail)
                                W("scalar", sACT, negc_act)
                            OP("scalar", "activation", inc=sACT,
                               out=EX[:, g * 1024:(g + 1) * 1024],
                               in_=wb, func=AF.Exp, bias=NEGC[:, :],
                               scale=RSQ)
                            exp_n[g] = actfence()
                            if g >= 1:
                                W("tensor", sACT, exp_n[g - 1])
                                if g == 1:
                                    W("tensor", sDVE, prev_recip)
                                    W("tensor", sDVE, prev_xnw)
                                for u in range(2):
                                    kc = 2 * (g - 1) + u
                                    mm(P[0:1, 2048:2560], ONESB[:, :],
                                       EX[:, kc * 512:(kc + 1) * 512],
                                       start=denom_start, stop=False, skip=True)
                                    denom_start = False
                                    mm(P[:, 3072:3584],
                                       Vt[:, kc * 128:(kc + 1) * 128],
                                       EX[:, kc * 512:(kc + 1) * 512],
                                       start=att_start, stop=False, skip=True)
                                    att_start = False
                        W("tensor", sACT, exp_n[15])
                        tail_pe = 0
                        for u in range(2):
                            kc = 30 + u
                            mm(P[0:1, 2048:2560], ONESB[:, :],
                               EX[:, kc * 512:(kc + 1) * 512],
                               start=False, stop=(u == 1), skip=True)
                            tail_pe = mm(P[:, 3072:3584],
                                         Vt[:, kc * 128:(kc + 1) * 128],
                                         EX[:, kc * 512:(kc + 1) * 512],
                                         start=False, stop=(u == 1),
                                         inc=(sPE if u == 1 else None), skip=True)
                        W("vector", sPE, tail_pe)
                        prev_recip = OP("vector", "reciprocal", inc=sDVE,
                                        out=ROWS[0:1, 512:1024],
                                        in_=P[0:1, 2048:2560])
                        W("tensor", sDVE, prev_recip)
                        W("tensor", sDVE, prev_rdbcp)
                        rb_pe = mm(P[:, 2560:3072], ONES1[:, :],
                                   ROWS[0:1, 512:1024], start=True, stop=True,
                                   inc=sPE)
                        W("vector", sPE, rb_pe)
                        prev_rdbcp = OP("vector", "tensor_copy", inc=sDVE, out=rdb,
                                        in_=P[:, 2560:3072])
                        W("vector", sDVE, prev_rdbcp)
                        prev_xnw = OP("vector", "tensor_tensor", inc=sDVE,
                                      out=xncol(qh, qt * TT, TT),
                                      in0=P[:, 3072:3584], in1=rdb, op=ALU.mult)
                        prev_tail = tail_pe
                        prev_exp_last = exp_n[15]
            if xn2_d is not None:
                W("gpsimd", sDVE, sDVE.n)
                W("gpsimd", sPE, sPE.n)
                for ic in range(NIC):
                    OP("gpsimd", "dma_start", inc=sST, amt=16, out=xn2_d[ic],
                       in_=xncol(ic, 0, TOWN))
            barrier()

        def stage3():
            WA = [A2f[:, 0:4096], A2f[:, 4096:8192]]
            AB = A2f[:, 8192:9216]
            scol = A2f[:, 9216:9217]
            scol2 = A2f[:, 9217:9218]
            SG = A2b[:, 20480:24576]
            OST = [A2f[:, 12288 + i * 512: 12288 + (i + 1) * 512] for i in range(2)]
            O2 = A2f[:, 13312:13824]
            RSUM = [A2f[0:1, 13824 + t * 512: 13824 + (t + 1) * 512]
                    for t in range(4)]
            RSQS = [A2f[0:1, 15872 + t * 512: 15872 + (t + 1) * 512]
                    for t in range(3)] + [A2f[0:1, 9728:10240]]
            ld_n = [0, 0]
            sign_act = 0
            abs_dve = 0
            proj_pe = {}
            evac = {}
            for oc in range(NH):
                sl = oc % 2
                if oc >= 2:
                    W("sync", sACT, max(sign_act, abs_dve))
                ld_n[sl] = OP("sync", "dma_start", inc=sLD[sl], amt=16,
                              out=WA[sl].rearrange("p (c k) -> p c k", c=NIC),
                              in_=wo[oc])
                W("scalar", sLD[sl], ld_n[sl])
                if oc >= 1:
                    W("scalar", sPE, proj_pe.get(oc - 1, 0))
                sign_act = OP("scalar", "activation", inc=sACT, out=SG[:, :],
                              in_=WA[sl][:, :], func=AF.Sign)
                qtr_pe = 0
                for qtr in range(4):
                    W("scalar", sPE, qtr_pe)
                    OP("scalar", "activation", inc=sACT, out=AB[:, :],
                       in_=WA[sl][:, qtr * 1024:(qtr + 1) * 1024],
                       func=AF.Abs)
                    abs_act = actfence()
                    W("tensor", sACT, abs_act)
                    for j in range(8):
                        ic = qtr * 8 + j
                        qtr_pe = mm(P[:, 2048:2049], AB[:, j * 128:(j + 1) * 128],
                                    ONESF[:, :], start=(ic == 0),
                                    stop=(ic == NIC - 1),
                                    inc=(sPE if j == 7 else None))
                abs_dve = abs_act
                sign_act = abs_act
                W("vector", sPE, sPE.n)
                d1 = OP("vector", "tensor_scalar", inc=sDVE, out=scol,
                        in0=P[:, 2048:2049], scalar1=INV_D, scalar2=1e-8,
                        op0=ALU.mult, op1=ALU.max)
                W("vector", sDVE, d1)
                d2 = OP("vector", "tensor_tensor", inc=sDVE, out=scol2, in0=scol,
                        in1=OS[:, oc:oc + 1], op=ALU.mult)
                for tt in range(NTT):
                    bank = P[:, (tt % 2) * 512:(tt % 2) * 512 + 512]
                    W("tensor", sACT, sign_act)
                    pk = (oc, tt - 2) if tt >= 2 else \
                        ((oc - 1, NTT - 2 + tt) if oc > 0 else None)
                    if pk is not None and pk in evac:
                        W("tensor", sDVE, evac[pk])
                    for ic in range(NIC):
                        proj_pe[oc] = mm(bank, SG[:, ic * 128:(ic + 1) * 128],
                                         xncol(ic, tt * TT, TT), start=(ic == 0),
                                         stop=(ic == NIC - 1),
                                         inc=(sPE if ic == NIC - 1 else None))
                    W("vector", sPE, proj_pe[oc])
                    W("vector", sDVE, d2)
                    if pk is not None:
                        W("vector", sST, sST.n)
                    osl = OST[tt % 2]
                    evac[(oc, tt)] = OP("vector", "tensor_scalar", inc=sDVE,
                                        out=osl, in0=bank, scalar1=scol2,
                                        scalar2=None, op0=ALU.mult)
                    W("gpsimd", sDVE, evac[(oc, tt)])
                    OP("gpsimd", "dma_start", inc=sST, amt=16, out=o_s[tt, oc], in_=osl)
                    W("vector", sDVE, evac[(oc, tt)])
                    o2d = OP("vector", "tensor_tensor", inc=sDVE, out=O2, in0=osl,
                             in1=osl, op=ALU.mult)
                    W("tensor", sDVE, o2d)
                    mm(P[0:1, 2560:3072], ONESF[:, :], osl, start=True, stop=True)
                    p2 = mm(P[0:1, 3072:3584], ONESF[:, :], O2, start=True,
                            stop=True, inc=sPE)
                    W("vector", sPE, p2)
                    if oc == 0:
                        a1 = OP("vector", "tensor_copy", inc=sDVE, out=RSUM[tt],
                                in_=P[0:1, 2560:3072])
                        W("vector", sDVE, a1)
                        OP("vector", "tensor_copy", inc=sDVE, out=RSQS[tt],
                           in_=P[0:1, 3072:3584])
                    else:
                        a1 = OP("vector", "tensor_tensor", inc=sDVE, out=RSUM[tt],
                                in0=RSUM[tt], in1=P[0:1, 2560:3072], op=ALU.add)
                        W("vector", sDVE, a1)
                        OP("vector", "tensor_tensor", inc=sDVE, out=RSQS[tt],
                           in0=RSQS[tt], in1=P[0:1, 3072:3584], op=ALU.add)
            barrier()
            ln_apply_residual(o_s, x_own, x2_s, RSUM, RSQS, store_out=False)
            barrier()

        def ln_apply_residual(src_s, base2d, dst_s, RSUM, RSQS, store_out):
            TLD = [A2f[:, i * 512:(i + 1) * 512] for i in range(4)]
            BLD = [A2f[:, 2048 + i * 512: 2048 + (i + 1) * 512] for i in range(4)]
            TMP = A2f[:, 4096:4608]
            OUTT = [A2f[:, 4608 + i * 512: 4608 + (i + 1) * 512] for i in range(2)]
            for tt in range(NTT):
                rm = ROWS[0:1, 0:512]
                rv = ROWS[0:1, 512:1024]
                rt = ROWS[0:1, 1024:1536]
                rvar = ROWS[0:1, 1536:2048]
                rln = ROWS[0:1, 512:1024]
                rstd = ROWS[0:1, 1024:1536]
                rt2 = ROWS[0:1, 1536:2048]
                W("vector", sDVE, sDVE.n)
                c1 = OP("vector", "tensor_scalar", inc=sDVE, out=rm, in0=RSUM[tt],
                        scalar1=INV_D, scalar2=None, op0=ALU.mult)
                c2 = OP("vector", "tensor_scalar", inc=sDVE, out=rv, in0=RSQS[tt],
                        scalar1=INV_D, scalar2=None, op0=ALU.mult)
                W("vector", sDVE, c2)
                c3 = OP("vector", "tensor_tensor", inc=sDVE, out=rt, in0=rm,
                        in1=rm, op=ALU.mult)
                W("vector", sDVE, c3)
                c4 = OP("vector", "tensor_tensor", inc=sDVE, out=rvar, in0=rv,
                        in1=rt, op=ALU.subtract)
                W("scalar", sDVE, c4)
                aa = OP("scalar", "activation", inc=sACT, out=rln, in_=rvar,
                        func=AF.Ln, bias=EPS1[0:1, 0:1])
                W("scalar", sACT, aa)
                OP("scalar", "activation", inc=sACT, out=rstd, in_=rln,
                   func=AF.Exp, scale=-0.5)
                a2 = actfence()
                W("vector", sACT, a2)
                c5 = OP("vector", "scalar_tensor_tensor", inc=sDVE, out=rt2, in0=rm,
                        scalar=-1.0, in1=rstd, op0=ALU.mult, op1=ALU.mult)
                W("tensor", sDVE, c5)
                mm(P[:, 3072:3584], ONES1[:, :], rstd, start=True, stop=True)
                bc = mm(P[:, 3584:4096], ONES1[:, :], rt2, start=True, stop=True,
                        inc=sPE)
                ld = [0, 0, 0, 0]
                ev = {}
                stn = {}

                def issue(fc2, tt=tt):
                    sl2 = fc2 % 4
                    OP("sync", "dma_start", inc=sLD[fc2 % 2], amt=16,
                       out=TLD[sl2], in_=src_s[tt, fc2])
                    ld[sl2] = OP("sync", "dma_start", inc=sLD[fc2 % 2], amt=16,
                                 out=BLD[sl2],
                                 in_=base2d[fc2 * 128:(fc2 + 1) * 128,
                                            tt * TT:(tt + 1) * TT])
                W("sync", sDVE, sDVE.n)
                for fc0 in range(3):
                    issue(fc0)
                for fc in range(NIC):
                    sl = fc % 4
                    if fc + 3 < NIC:
                        if fc >= 1 and (fc - 1) in ev:
                            W("sync", sDVE, ev[fc - 1])
                        issue(fc + 3)
                    W("vector", sLD[fc % 2], ld[sl])
                    W("vector", sPE, bc)
                    t1 = OP("vector", "tensor_tensor", inc=sDVE, out=TMP,
                            in0=TLD[sl], in1=P[:, 3072:3584], op=ALU.mult)
                    W("vector", sDVE, t1)
                    t2 = OP("vector", "tensor_tensor", inc=sDVE, out=TMP,
                            in0=TMP, in1=P[:, 3584:4096], op=ALU.add)
                    W("vector", sDVE, t2)
                    if fc >= 2 and (fc - 2) in stn:
                        W("vector", sST, stn[fc - 2])
                    ev[fc] = OP("vector", "tensor_tensor", inc=sDVE,
                                out=OUTT[fc % 2], in0=BLD[sl], in1=TMP, op=ALU.add)
                    W("gpsimd", sDVE, ev[fc])
                    if store_out:
                        stn[fc] = OP("gpsimd", "dma_start", inc=sST, amt=16,
                                     out=out_t[fc * 128:(fc + 1) * 128,
                                               tt * TT:(tt + 1) * TT],
                                     in_=OUTT[fc % 2])
                    else:
                        stn[fc] = OP("gpsimd", "dma_start", inc=sST, amt=16,
                                     out=dst_s[tt, fc], in_=OUTT[fc % 2])

        def stage4a():
            for tt in range(NTT):
                def load_fn(tt=tt):
                    OP("sync", "dma_start", inc=sXT, amt=16,
                       out=A2f[:, 0:8192].rearrange("p (c t) -> p c t", c=16),
                       in_=x2_s[tt, 0:16].rearrange("c p t -> p c t"))
                    return OP("sync", "dma_start", inc=sXT, amt=16,
                              out=A2f[:, 8192:16384].rearrange(
                                  "p (c t) -> p c t", c=16),
                              in_=x2_s[tt, 16:32].rearrange("c p t -> p c t"))
                W("sync", sDVE, sDVE.n)
                W("sync", sPE, sPE.n)
                W("sync", sST, sST.n)
                ln_tile(load_fn, tt * 512)
            barrier()

        def stage4b():
            WG = [A2b[:, 0:4096], A2b[:, 4096:8192]]
            WU = [A2b[:, 8192:12288], A2b[:, 12288:16384]]
            SGT = A2b[:, 16384:16896]
            STH = [A2b[:, 16896 + i * 512: 16896 + (i + 1) * 512] for i in range(2)]
            gld = [0, 0]
            uld = [0, 0]
            pe_n = {}
            hmul = {}
            stn = {}
            sil = {}
            for oc in range(NFF):
                sl = oc % 2
                if oc >= 2:
                    W("gpsimd", sPE, pe_n.get((oc - 2, NTT - 1), 0))
                gld[sl] = OP("gpsimd", "dma_start", inc=sGC[sl], amt=16,
                             out=WG[sl].rearrange("p (c k) -> p c k", c=NIC),
                             in_=wg[oc])
                uld[sl] = OP("gpsimd", "dma_start", inc=sGC[sl], amt=16,
                             out=WU[sl].rearrange("p (c k) -> p c k", c=NIC),
                             in_=wu[oc])
                for tt in range(NTT):
                    gb = P[:, (tt % 2) * 512:(tt % 2) * 512 + 512]
                    ub = P[:, 1024 + (tt % 2) * 512:1024 + (tt % 2) * 512 + 512]
                    W("tensor", sGC[sl], uld[sl])
                    pk = (oc, tt - 2) if tt >= 2 else \
                        ((oc - 1, NTT - 2 + tt) if oc > 0 else None)
                    if pk is not None:
                        if pk in sil:
                            W("tensor", sACT, sil[pk])
                        if pk in hmul:
                            W("tensor", sDVE, hmul[pk])
                    for ic in range(NIC):
                        mm(gb, WG[sl][:, ic * 128:(ic + 1) * 128],
                           xncol(ic, tt * TT, TT), start=(ic == 0),
                           stop=(ic == NIC - 1))
                    for ic in range(NIC):
                        pe_n[(oc, tt)] = mm(ub, WU[sl][:, ic * 128:(ic + 1) * 128],
                                            xncol(ic, tt * TT, TT),
                                            start=(ic == 0), stop=(ic == NIC - 1),
                                            inc=(sPE if ic == NIC - 1 else None))
                    W("scalar", sPE, pe_n[(oc, tt)])
                    if pk is not None and pk in hmul:
                        W("scalar", sDVE, hmul[pk])
                    OP("scalar", "activation", inc=sACT, out=SGT,
                       in_=gb, func=AF.Silu)
                    sil[(oc, tt)] = OP("scalar", "activation", inc=sACT,
                                       out=CONSTT[0:1, 2:3],
                                       in_=CONSTT[0:1, 0:1], func=AF.Copy)
                    W("vector", sACT, sil[(oc, tt)])
                    if pk is not None and pk in stn:
                        W("vector", sST, stn[pk])
                    hmul[(oc, tt)] = OP("vector", "tensor_tensor", inc=sDVE,
                                        out=STH[tt % 2], in0=SGT, in1=ub,
                                        op=ALU.mult)
                    W("gpsimd", sDVE, hmul[(oc, tt)])
                    stn[(oc, tt)] = OP("gpsimd", "dma_start", inc=sST, amt=16,
                                       out=h_s[tt, :, oc, :], in_=STH[tt % 2])
            barrier()

        def stage4c():
            WD = [A2b[:, 0:11008], A2b[:, 11008:22016]]
            YST = [A2f[:, 11008 + i * 512: 11008 + (i + 1) * 512] for i in range(2)]
            Y2 = A2f[:, 12032:12544]
            HT = XNa[:, 0:NFF * TT]
            for tt in range(NTT):
                W("sync", sST, sST.n)
                W("sync", sPE, sPE.n)
                hln = OP("sync", "dma_start", inc=sXT, amt=16,
                         out=HT.rearrange("p (c t) -> p c t", c=NFF),
                         in_=h_s[tt])
                wld = [0, 0]
                pe_n = {}
                ev = {}
                for oc in range(NIC):
                    sl = oc % 2
                    if oc >= 2:
                        W("gpsimd", sPE, pe_n.get(oc - 2, 0))
                    wld[sl] = OP("gpsimd", "dma_start", inc=sGC[sl], amt=16,
                                 out=WD[sl].rearrange("p (c k) -> p c k", c=NFF),
                                 in_=wd[oc])
                    yb = P[:, (oc % 2) * 512:(oc % 2) * 512 + 512]
                    W("tensor", sGC[sl], wld[sl])
                    if oc == 0:
                        W("tensor", sXT, hln)
                    if oc >= 2 and (oc - 2) in ev:
                        W("tensor", sDVE, ev[oc - 2])
                    for ic in range(NFF):
                        pe_n[oc] = mm(yb, WD[sl][:, ic * 128:(ic + 1) * 128],
                                      HT[:, ic * TT:(ic + 1) * TT],
                                      start=(ic == 0), stop=(ic == NFF - 1),
                                      inc=(sPE if ic == NFF - 1 else None))
                    W("vector", sPE, pe_n[oc])
                    if oc >= 2:
                        W("vector", sST, sST.n)
                    ysl = YST[oc % 2]
                    ev[oc] = OP("vector", "tensor_copy", inc=sDVE, out=ysl, in_=yb)
                    W("gpsimd", sDVE, ev[oc])
                    OP("gpsimd", "dma_start", inc=sST, amt=16, out=y_s[tt, oc],
                       in_=ysl)
                    W("vector", sDVE, ev[oc])
                    y2d = OP("vector", "tensor_tensor", inc=sDVE, out=Y2, in0=ysl,
                             in1=ysl, op=ALU.mult)
                    W("tensor", sDVE, y2d)
                    mm(P[0:1, 2048:2560], ONESF[:, :], ysl,
                       start=(oc == 0), stop=(oc == NIC - 1), skip=True)
                    mm(P[0:1, 2560:3072], ONESF[:, :], Y2,
                       start=(oc == 0), stop=(oc == NIC - 1),
                       inc=(sPE if oc == NIC - 1 else None), skip=True)
                stats_pe = sPE.n
                W("vector", sPE, stats_pe)
                cs = OP("vector", "tensor_copy", inc=sDVE, out=ROWS[0:1, 0:512],
                        in_=P[0:1, 2048:2560])
                W("vector", sDVE, cs)
                OP("vector", "tensor_copy", inc=sDVE, out=ROWS[0:1, 512:1024],
                   in_=P[0:1, 2560:3072])
                ln_apply_one_tt(tt)
            barrier()

        def ln_apply_one_tt(tt):
            rm = ROWS[0:1, 0:512]
            rv = ROWS[0:1, 512:1024]
            rt = ROWS[0:1, 1024:1536]
            rvar = ROWS[0:1, 1536:2048]
            rln = ROWS[0:1, 512:1024]
            rstd = ROWS[0:1, 1024:1536]
            rt2 = ROWS[0:1, 1536:2048]
            W("vector", sDVE, sDVE.n)
            c1 = OP("vector", "tensor_scalar", inc=sDVE, out=rm, in0=rm,
                    scalar1=INV_D, scalar2=None, op0=ALU.mult)
            c2 = OP("vector", "tensor_scalar", inc=sDVE, out=rv, in0=rv,
                    scalar1=INV_D, scalar2=None, op0=ALU.mult)
            W("vector", sDVE, c2)
            c3 = OP("vector", "tensor_tensor", inc=sDVE, out=rt, in0=rm, in1=rm,
                    op=ALU.mult)
            W("vector", sDVE, c3)
            c4 = OP("vector", "tensor_tensor", inc=sDVE, out=rvar, in0=rv, in1=rt,
                    op=ALU.subtract)
            W("scalar", sDVE, c4)
            aa = OP("scalar", "activation", inc=sACT, out=rln, in_=rvar, func=AF.Ln,
                    bias=EPS1[0:1, 0:1])
            W("scalar", sACT, aa)
            OP("scalar", "activation", inc=sACT, out=rstd, in_=rln, func=AF.Exp,
               scale=-0.5)
            a2 = actfence()
            W("vector", sACT, a2)
            c5 = OP("vector", "scalar_tensor_tensor", inc=sDVE, out=rt2, in0=rm,
                    scalar=-1.0, in1=rstd, op0=ALU.mult, op1=ALU.mult)
            W("tensor", sDVE, c5)
            mm(P[:, 3072:3584], ONES1[:, :], rstd, start=True, stop=True)
            bc = mm(P[:, 3584:4096], ONES1[:, :], rt2, start=True, stop=True,
                    inc=sPE)
            TLD = [A2f[:, 12544 + i * 512: 12544 + (i + 1) * 512] for i in range(4)]
            BLD = [A2f[:, 14592 + i * 512: 14592 + (i + 1) * 512] for i in range(4)]
            TMP = A2f[:, 12032:12544]
            OUTT = [A2f[:, 11008 + i * 512: 11008 + (i + 1) * 512] for i in range(2)]
            ld = [0, 0, 0, 0]
            ev = {}
            stn = {}
            W("sync", sST, sST.n)

            def issue5(fc2, tt=tt):
                sl2 = fc2 % 4
                OP("sync", "dma_start", inc=sLD[fc2 % 2], amt=16,
                   out=TLD[sl2], in_=y_s[tt, fc2])
                ld[sl2] = OP("sync", "dma_start", inc=sLD[fc2 % 2], amt=16,
                             out=BLD[sl2], in_=x2_s[tt, fc2])
            W("sync", sDVE, sDVE.n)
            for fc0 in range(3):
                issue5(fc0)
            for fc in range(NIC):
                sl = fc % 4
                if fc + 3 < NIC:
                    if fc >= 1 and (fc - 1) in ev:
                        W("sync", sDVE, ev[fc - 1])
                    issue5(fc + 3)
                W("vector", sLD[fc % 2], ld[sl])
                W("vector", sPE, bc)
                t1 = OP("vector", "tensor_tensor", inc=sDVE, out=TMP,
                        in0=TLD[sl], in1=P[:, 3072:3584], op=ALU.mult)
                W("vector", sDVE, t1)
                t2 = OP("vector", "tensor_tensor", inc=sDVE, out=TMP,
                        in0=TMP, in1=P[:, 3584:4096], op=ALU.add)
                W("vector", sDVE, t2)
                if fc >= 2 and (fc - 2) in stn:
                    W("vector", sST, stn[fc - 2])
                ev[fc] = OP("vector", "tensor_tensor", inc=sDVE,
                            out=OUTT[fc % 2], in0=BLD[sl], in1=TMP, op=ALU.add)
                W("gpsimd", sDVE, ev[fc])
                stn[fc] = OP("gpsimd", "dma_start", inc=sST, amt=16,
                             out=out_t[fc * 128:(fc + 1) * 128,
                                       tt * TT:(tt + 1) * TT], in_=OUTT[fc % 2])

        if upto >= 0.5:
            stage1()
        if upto >= 2:
            stage2()
        if upto >= 3:
            stage3()
        if upto >= 4:
            stage4a()
            stage4b()
            stage4c()
        for eng in ENGS:
            W(eng, sST, sST.n)
        barrier()

        block = stk.enter_context(nc.Block())

        @block.sync
        def _(e):
            for f in REC["sync"]:
                f(e)

        @block.gpsimd
        def _(e):
            for f in REC["gpsimd"]:
                f(e)

        @block.vector
        def _(e):
            for f in REC["vector"]:
                f(e)

        @block.scalar
        def _(e):
            for f in REC["scalar"]:
                f(e)

        @block.tensor
        def _(e):
            for f in REC["tensor"]:
                f(e)

    return nc


def _layout_inputs(inputs):
    x = np.asarray(inputs["L0_attn_hidden_states"])
    Wq = np.asarray(inputs["L0_attn_q_weight"])
    Wk = np.asarray(inputs["L0_attn_k_weight"])
    Wv = np.asarray(inputs["L0_attn_v_weight"])
    Wo = np.asarray(inputs["L0_attn_o_weight"])
    Wg = np.asarray(inputs["L0_ffn_W_gate"])
    Wu = np.asarray(inputs["L0_ffn_W_up"])
    Wd = np.asarray(inputs["L0_ffn_W_down"])

    def lhst_tiles(Wm, noc, nic):
        return np.ascontiguousarray(
            Wm.reshape(noc, 128, nic, 128).transpose(0, 3, 2, 1))

    shared = {
        "wq": lhst_tiles(Wq, NH, NIC),
        "wk": lhst_tiles(Wk, NKV, NIC),
        "wv": np.ascontiguousarray(Wv.T).reshape(NIC, 128, 1024),
        "wo": lhst_tiles(Wo, NH, NIC),
        "wg": lhst_tiles(Wg, NFF, NIC),
        "wu": lhst_tiles(Wu, NFF, NIC),
        "wd": np.ascontiguousarray(
            Wd.reshape(NIC, 128, NFF, 128).transpose(0, 3, 2, 1)),
        "qs": np.ascontiguousarray(
            np.asarray(inputs["L0_attn_q_scale"]).reshape(NH, 128).T),
        "ks": np.ascontiguousarray(
            np.asarray(inputs["L0_attn_k_scale"]).reshape(NKV, 128).T),
        "vs": np.asarray(inputs["L0_attn_v_scale"]).reshape(1, 1024).copy(),
        "os": np.ascontiguousarray(
            np.asarray(inputs["L0_attn_o_scale"]).reshape(NH, 128).T),
    }
    in_maps = []
    for c in range(8):
        seq, half = c // 2, c % 2
        fm = np.ascontiguousarray(x[seq].T)
        m = dict(shared)
        m["x_own"] = np.ascontiguousarray(fm[:, half * TOWN:(half + 1) * TOWN])
        m["x_oth"] = np.ascontiguousarray(fm[:, (1 - half) * TOWN:(2 - half) * TOWN])
        in_maps.append(m)
    return in_maps


def kernel(**inputs):
    in_maps = _layout_inputs(inputs)
    nc = build_program()
    res = run_bass_kernel_spmd(nc, in_maps, core_ids=list(range(8)))
    out = np.empty((4, SEQ, D), dtype=np.float32)
    for c in range(8):
        seq, half = c // 2, c % 2
        out[seq, half * TOWN:(half + 1) * TOWN, :] = res.results[c]["out"].T
    return out
